# version 01 (final)
# baseline (speedup 1.0000x reference)
import numpy as np
import concourse.bass as bass
import concourse.mybir as mybir
from concourse.bass_utils import run_bass_kernel_spmd
from concourse.bass_core_rust import IndirectOffsetOnAxis

F32 = mybir.dt.float32
BF16 = mybir.dt.bfloat16
I32 = mybir.dt.int32
ALU = mybir.AluOpType
ACT = mybir.ActivationFunctionType
AX = mybir.AxisListType

NCORES = 8
T = 2048
D = 1024
DC = D // 128
HEADS = 16
HD = 64
EPS = 1e-5
E0, H0, TOPK = 64, 256, 8
ES, HS = 2, 2048
INNER, DSTATE, DTRANK, KSZ = 512, 8, 64, 3
E1, H1, CAP = 8, 4096, 512


class Sem:
    def __init__(self, handle, name):
        self.h = handle
        self.name = name
        self.n = 0


class _Op:
    __slots__ = ("fn", "waits", "incs", "attachable", "sig")

    def __init__(self, fn, waits, incs, attachable, sig=None):
        self.fn = fn
        self.waits = waits
        self.incs = incs
        self.attachable = attachable
        self.sig = sig


class Prog:

    ENGINES = ("sync", "tensor", "vector", "scalar", "gpsimd")

    def __init__(self, nc):
        self.nc = nc
        self.q = {e: [] for e in self.ENGINES}
        self.dirty = {e: False for e in self.ENGINES}
        self.bar = None
        self.nops = 0

    def sem(self, name):
        return Sem(self.nc.semaphore(name).__enter__(), name)

    def on(self, eng, fn, waits=(), inc=None, attachable=True):
        incs = []
        sig = None
        if inc is not None:
            inc[0].n += inc[1]
            incs.append(inc)
            sig = (inc[0], inc[0].n)
        op = _Op(fn, tuple(waits), incs, attachable, sig)
        self.q[eng].append(op)
        self.dirty[eng] = True
        self.nops += 1
        return op

    def dma(self, eng, fn, waits=(), inc=None):
        return self.on(eng, fn, waits=waits, inc=inc, attachable=False)

    def wait(self, eng, sem, v):
        self.q[eng].append(_Op(None, ((sem, v),), [], False))

    def barrier(self, engines=("tensor", "vector", "scalar", "gpsimd")):
        if self.bar is None:
            self.bar = self.sem("bar")
        signals = []
        for e in engines:
            if not self.dirty[e]:
                continue
            ops = self.q[e]
            tgt = None
            for op in reversed(ops):
                if op.fn is not None:
                    tgt = op
                    break
            if tgt is None:
                continue
            if tgt.sig is not None:
                signals.append(tgt.sig)
            elif tgt.attachable and not tgt.incs:
                self.bar.n += 1
                tgt.incs.append((self.bar, 1))
                tgt.sig = (self.bar, self.bar.n)
                signals.append(tgt.sig)
            else:
                if e == "tensor":
                    raise RuntimeError("tensor barrier: last op not attachable")
                self.bar.n += 1
                op = _Op(
                    (lambda en: en.engine_nop() if hasattr(en, "engine_nop") else en.nop()),
                    (), [(self.bar, 1)], True, (self.bar, self.bar.n))
                self.q[e].append(op)
                signals.append(op.sig)
        for e in engines:
            for (s2, v) in signals:
                self.wait(e, s2, v)
            self.dirty[e] = False
        return signals

    def finalize(self, block):
        for name in self.ENGINES:
            ops = self.q[name]

            def body(e, ops=ops):
                for op in ops:
                    for (s, v) in op.waits:
                        e.wait_ge(s.h, v)
                    if op.fn is None:
                        continue
                    r = op.fn(e)
                    for (s, k) in op.incs:
                        r = r.then_inc(s.h, k)
            getattr(block, name)(body)


def _bcast_col(p, t, ps, s_mm, src_row_ap, bank, n, waits=()):
    ones1 = t["onesf"].ap()[0:1, 0:128]
    return p.on("tensor", lambda e: e.matmul(
        ps[:, bank, 0:n], ones1, src_row_ap, start=True, stop=True),
        waits=waits, inc=(s_mm, 1))


_RESID_HBM = [None]
_SLOT_DRAM = [None]
_XN_TM = [None, None]
_SSM_HBM = {}


def build_program(phase_limit=99, dump=(), n_moe=E0, gate_parts=9, ssm_parts=9, scan_tiles=32, ffn_on=True):
    nc = bass.Bass("TRN2", target_bir_lowering=False,
                   dynamic_dma_scratch_size=16384)

    X = nc.dram_tensor("L0_attn_x", [T, D], F32, kind="ExternalInput").ap()
    qkv_w = nc.dram_tensor("L0_attn_qkv_w", [3 * D, D], F32, kind="ExternalInput").ap()
    qkv_b = nc.dram_tensor("L0_attn_qkv_b", [3 * D], F32, kind="ExternalInput").ap()
    out_w = nc.dram_tensor("L0_attn_out_w", [D, D], F32, kind="ExternalInput").ap()
    out_b = nc.dram_tensor("L0_attn_out_b", [D], F32, kind="ExternalInput").ap()
    router_w = nc.dram_tensor("L0_ffn_router_w", [D, E0], F32, kind="ExternalInput").ap()
    router_b = nc.dram_tensor("L0_ffn_router_b", [E0], F32, kind="ExternalInput").ap()
    moe_w1 = nc.dram_tensor("L0_ffn_w1", [E0, D, H0], F32, kind="ExternalInput").ap()
    moe_b1 = nc.dram_tensor("L0_ffn_b1", [E0, H0], F32, kind="ExternalInput").ap()
    moe_w2 = nc.dram_tensor("L0_ffn_w2", [E0, H0, D], F32, kind="ExternalInput").ap()
    moe_b2 = nc.dram_tensor("L0_ffn_b2", [E0, D], F32, kind="ExternalInput").ap()
    sw1 = nc.dram_tensor("L0_ffn_sw1", [ES, D, HS], F32, kind="ExternalInput").ap()
    sb1 = nc.dram_tensor("L0_ffn_sb1", [ES, HS], F32, kind="ExternalInput").ap()
    sw2 = nc.dram_tensor("L0_ffn_sw2", [ES, HS, D], F32, kind="ExternalInput").ap()
    sb2 = nc.dram_tensor("L0_ffn_sb2", [ES, D], F32, kind="ExternalInput").ap()
    W_in = nc.dram_tensor("L1_attn_W_in", [D, D], F32, kind="ExternalInput").ap()
    KxI = nc.dram_tensor("L1_attn_Kx", [INNER, 1, KSZ], F32, kind="ExternalInput").ap()
    KzI = nc.dram_tensor("L1_attn_Kz", [INNER, 1, KSZ], F32, kind="ExternalInput").ap()
    W_xproj = nc.dram_tensor("L1_attn_W_xproj", [80, INNER], F32, kind="ExternalInput").ap()
    W_dt = nc.dram_tensor("L1_attn_W_dt", [INNER, DTRANK], F32, kind="ExternalInput").ap()
    b_dt = nc.dram_tensor("L1_attn_b_dt", [INNER], F32, kind="ExternalInput").ap()
    A_log = nc.dram_tensor("L1_attn_A_log", [INNER, DSTATE], F32, kind="ExternalInput").ap()
    Dvec = nc.dram_tensor("L1_attn_D", [INNER], F32, kind="ExternalInput").ap()
    W_out = nc.dram_tensor("L1_attn_W_out", [D, D], F32, kind="ExternalInput").ap()
    gate_w = nc.dram_tensor("L1_ffn_gate_w", [D, E1], F32, kind="ExternalInput").ap()
    ffn1_w1 = nc.dram_tensor("L1_ffn_w1", [E1, D, H1], F32, kind="ExternalInput").ap()
    ffn1_w2 = nc.dram_tensor("L1_ffn_w2", [E1, H1, D], F32, kind="ExternalInput").ap()

    OUT = nc.dram_tensor("out", [T, D], F32, kind="ExternalOutput").ap()
    xn_tm_hi = nc.dram_tensor("xn_tm_hi", [T, D], BF16, kind="Internal").ap()
    xn_tm_lo = nc.dram_tensor("xn_tm_lo", [T, D], BF16, kind="Internal").ap()
    slot_dram = nc.dram_tensor("slot_dram", [E1, 2560, 2], F32, kind="Internal").ap()
    _SLOT_DRAM[0] = slot_dram
    _XN_TM[0] = xn_tm_hi
    _XN_TM[1] = xn_tm_lo
    xb_hbm = nc.dram_tensor("xb_hbm", [INNER, T], F32, kind="Internal").ap()
    delta_hbm = nc.dram_tensor("delta_hbm", [INNER, T], F32, kind="Internal").ap()
    zb_hbm_hi = nc.dram_tensor("zb_hbm_hi", [INNER, T], BF16, kind="Internal").ap()
    zb_hbm_lo = nc.dram_tensor("zb_hbm_lo", [INNER, T], BF16, kind="Internal").ap()
    y_hbm_hi = nc.dram_tensor("y_hbm_hi", [INNER, T], BF16, kind="Internal").ap()
    y_hbm_lo = nc.dram_tensor("y_hbm_lo", [INNER, T], BF16, kind="Internal").ap()
    resid_hbm = nc.dram_tensor("resid_hbm", [D, T], F32, kind="Internal").ap()
    _RESID_HBM[0] = resid_hbm
    _SSM_HBM.clear()
    _SSM_HBM.update(xb=xb_hbm, delta=delta_hbm, zbh=zb_hbm_hi, zbl=zb_hbm_lo,
                    yh=y_hbm_hi, yl=y_hbm_lo)
    ffn_acc = nc.dram_tensor("ffn_acc", [T, D], F32, kind="Internal").ap()

    sb = nc.alloc_sbuf_tensor
    t = {}
    t["xn_hi"] = sb("xn_hi", [128, DC, T], BF16)
    t["xn_lo"] = sb("xn_lo", [128, DC, T], BF16)
    t["acc"] = sb("acc", [128, DC, T], F32)
    t["pool"] = sb("pool", [128, 15872], F32)
    t["ident"] = sb("ident", [128, 128], F32)
    t["onesf"] = sb("onesf", [64, 512], F32)
    t["onesb"] = sb("onesb", [128, 1], BF16)
    t["pidx"] = sb("pidx", [128, 1], F32)
    t["Eh"] = sb("Eh", [128, 2], BF16)
    t["onesbr"] = sb("onesbr", [1, 128], BF16)
    t["identb"] = sb("identb", [128, 128], BF16)
    t["kco"] = sb("kco", [128, 33], F32)
    t["brow"] = sb("brow", [1, 1600], F32)
    t["bias"] = sb("bias", [128, 8], F32)

    class _V:
        def __init__(self, a):
            self._a = a

        def ap(self):
            return self._a

    def pv(off, shape, dtype=F32):
        n = 1
        for d in shape[1:]:
            n *= d
        if dtype == F32:
            assert off + n <= 15872, (off, shape)
            a = t["pool"].ap()[:, off:off + n]
        elif dtype == I32:
            assert off + n <= 15872, (off, shape)
            a = t["pool"].ap().bitcast(I32)[:, off:off + n]
        else:
            assert dtype == BF16 and off * 2 + n <= 31744
            a = t["pool"].ap().bitcast(BF16)[:, off * 2:off * 2 + n]
        if len(shape) > 2:
            names = " ".join(f"d{i}" for i in range(1, len(shape)))
            a = a.rearrange(f"p ({names}) -> p {names}",
                            **{f"d{i}": shape[i] for i in range(1, len(shape))})
        return a[0:shape[0]] if shape[0] != 128 else a

    t["row0"] = sb("row0", [1, 2048], F32)
    t["scr"] = _V(pv(12288, [128, 2048]))
    t["sqh"] = _V(pv(11264, [128, 2, 512], BF16))
    t["sql"] = _V(pv(10240, [128, 2, 512], BF16))

    tens = dict(t)
    tens["nc"] = nc
    tens["pv"] = pv

    with (
        nc.psum_tensor("ps", [128, 8, 512], F32) as ps_t,
        nc.Block() as block,
    ):
        ps = ps_t.ap()
        p = Prog(nc)
        s_ld = p.sem("s_ld")
        s_pr = p.sem("s_pr")
        s_mm = p.sem("s_mm")
        s_ev = p.sem("s_ev")
        s_sc = p.sem("s_sc")
        s_st = p.sem("s_st")
        s_wl = p.sem("s_wl")
        s_gp = p.sem("s_gp")
        sems = dict(ld=s_ld, pr=s_pr, mm=s_mm, ev=s_ev, sc=s_sc, st=s_st, wl=s_wl, gp=s_gp)

        ident = t["ident"].ap()

        p.on("vector", lambda e: e.memset(t["onesf"].ap()[:, :], 1.0))
        p.on("vector", lambda e: e.memset(t["onesb"].ap()[:, :], 1.0))
        p.on("vector", lambda e: e.memset(t["onesbr"].ap()[:, :], 1.0))
        p.on("vector", lambda e: e.memset(t["kco"].ap()[:, 32:33], 1.0))
        p.on("vector", lambda e: e.memset(t["row0"].ap()[0:1, 0:1], 0.0),
             inc=(s_ev, 1))
        for k in [1, 2, 4, 8, 16, 32, 64, 128, 256, 512, 1024]:
            p.on("vector", lambda e, k=k: e.tensor_scalar(
                t["row0"].ap()[0:1, k:2 * k], t["row0"].ap()[0:1, 0:k], float(k), None, ALU.add),
                waits=((s_ev, s_ev.n),), inc=(s_ev, 1))
        EV_IOTA = s_ev.n
        p.on("tensor", lambda e: e.matmul(
            ps[:, 6, 0:1], t["row0"].ap()[0:1, 0:128], t["onesf"].ap()[0:1, 0:1],
            start=True, stop=True),
            waits=((s_ev, EV_IOTA),), inc=(s_mm, 1))
        MM_PIDX = s_mm.n
        p.on("vector", lambda e: e.tensor_copy(t["pidx"].ap(), ps[:, 6, 0:1]),
             waits=((s_mm, MM_PIDX),), inc=(s_ev, 1))
        EV_PIDX = s_ev.n
        _bcast_col(p, t, ps, s_mm, t["row0"].ap()[0:1, 0:128], 7, 128)
        MM_IOB = s_mm.n
        p.on("vector", lambda e: e.tensor_scalar(
            ident, ps[:, 7, 0:128], t["pidx"].ap()[:, 0:1], None, ALU.is_equal),
            waits=((s_mm, MM_IOB), (s_ev, EV_PIDX)), inc=(s_ev, 1))
        p.on("vector", lambda e: e.tensor_scalar(
            t["Eh"].ap()[:, 1:2], t["pidx"].ap()[:, 0:1], 64.0, None, ALU.is_ge))
        p.on("vector", lambda e: e.tensor_scalar(
            t["Eh"].ap()[:, 0:1], t["pidx"].ap()[:, 0:1], 64.0, None, ALU.is_lt),
            inc=(s_ev, 1))

        xtile = pv(0, [128, 2, D])
        stage = pv(2048, [128, 2, DC, 128])
        LD0 = s_ld.n

        def xload(tc, waits):
            p.dma("sync", lambda e, tc=tc: e.dma_start(
                out=xtile[:, tc % 2, :], in_=X[128 * tc:128 * (tc + 1), :]),
                waits=waits, inc=(s_ld, 16))

        EV_BASE = s_ev.n
        xload(0, ((s_ev, EV_BASE),))
        xload(1, ())
        for tc in range(16):
            for c in range(DC):
                i = tc * DC + c
                p.on("tensor", lambda e, tc=tc, c=c: e.matmul(
                    ps[:, c % 4, 0:128], xtile[:, tc % 2, 128 * c:128 * (c + 1)],
                    ident, start=True, stop=True),
                    waits=((s_ld, LD0 + 16 * (tc + 1)),)
                    + (((s_ev, EV_BASE + i - 3),) if i >= 3 else ()),
                    inc=(s_mm, 1))
                w2_ = ((s_st, 16 * (tc - 1)),) if (c == 0 and tc >= 2) else ()
                p.on("vector", lambda e, tc=tc, c=c: e.tensor_copy(
                    stage[:, tc % 2, c, :], ps[:, c % 4, 0:128]),
                    waits=((s_mm, s_mm.n),) + w2_,
                    inc=(s_ev, 1))
            p.dma("sync", lambda e, tc=tc: e.dma_start(
                out=resid_hbm.rearrange("(c p) t -> p c t", p=128)[:, :, 128 * tc:128 * (tc + 1)],
                in_=stage[:, tc % 2, :, :]),
                waits=((s_ev, s_ev.n),), inc=(s_st, 16))
            if tc + 2 < 16:
                xload(tc + 2, ((s_ev, EV_BASE + (tc + 1) * DC),))
        p.barrier()
        p.wait("sync", s_st, s_st.n)
        ST_RESID = s_st.n

        def emit_layernorm(ld_wait):
            rchunk = pv(4096, [128, DC, 512])
            for tg in range(4):
                a, b = 512 * tg, 512 * (tg + 1)
                LDW = s_ld.n
                p.dma("sync", lambda e, a=a, b=b: e.dma_start(
                    out=rchunk, in_=resid_hbm.rearrange("(c p) t -> p c t", p=128)[:, :, a:b]),
                    waits=((ld_wait[0], ld_wait[1]),) + (((s_ev, s_ev.n),) if tg > 0 else ()),
                    inc=(s_ld, 16))
                LDR = s_ld.n
                for c in range(DC):
                    p.on("vector", lambda e, c=c, a=a, b=b: e.tensor_copy(
                        t["xn_hi"].ap()[:, c, a:b], rchunk[:, c, :]),
                        waits=((s_ld, LDR),) if c == 0 else ())
                    p.on("vector", lambda e, c=c, a=a, b=b: e.tensor_sub(
                        t["xn_lo"].ap()[:, c, a:b], rchunk[:, c, :],
                        t["xn_hi"].ap()[:, c, a:b]), inc=(s_ev, 1))
                EV_S = s_ev.n
                for c in range(DC):
                    for k, src in enumerate(("xn_hi", "xn_lo")):
                        st_, sp_ = (c == 0 and k == 0), (c == DC - 1 and k == 1)
                        p.on("tensor", lambda e, src=src, c=c, a=a, b=b,
                             st=st_, sp=sp_: e.matmul(
                                 ps[0:1, 4, :], t["onesb"].ap(),
                                 t[src].ap()[:, c, a:b], start=st, stop=sp),
                             waits=((s_ev, EV_S),) if st_ else (),
                             inc=(s_mm, 1) if sp_ else None)
                for c in range(DC):
                    par = c % 2
                    MM_PREV = s_mm.n
                    p.on("vector", lambda e, c=c, a=a, b=b: e.tensor_mul(
                        t["scr"].ap()[:, 0:512], rchunk[:, c, :], rchunk[:, c, :]),
                        waits=((s_mm, MM_PREV),) if c >= 2 else ())
                    p.on("vector", lambda e, par=par: e.tensor_copy(
                        t["sqh"].ap()[:, par, :], t["scr"].ap()[:, 0:512]))
                    p.on("vector", lambda e, par=par: e.tensor_sub(
                        t["sql"].ap()[:, par, :], t["scr"].ap()[:, 0:512],
                        t["sqh"].ap()[:, par, :]),
                        inc=(s_ev, 1))
                    EV_Q = s_ev.n
                    for k, srcp in enumerate(("sqh", "sql")):
                        p.on("tensor", lambda e, srcp=srcp, c=c, par=par,
                             st=(c == 0 and k == 0), sp=(c == DC - 1 and k == 1): e.matmul(
                                 ps[0:1, 5, :], t["onesb"].ap(), t[srcp].ap()[:, par, :],
                                 start=st, stop=sp),
                             waits=((s_ev, EV_Q),) if k == 0 else (),
                             inc=(s_mm, 1) if k == 1 else None)
                MM_QS = s_mm.n
                r0 = t["row0"].ap()
                p.on("vector", lambda e: e.tensor_scalar(
                    r0[0:1, 0:512], ps[0:1, 4, :], 1.0 / D, None, ALU.mult),
                    waits=((s_mm, MM_QS),), inc=(s_ev, 1))
                p.on("vector", lambda e: e.tensor_scalar(
                    r0[0:1, 512:1024], ps[0:1, 5, :], 1.0 / D, None, ALU.mult),
                    waits=((s_ev, s_ev.n),), inc=(s_ev, 1))
                p.on("vector", lambda e: e.tensor_mul(
                    r0[0:1, 1536:2048], r0[0:1, 0:512], r0[0:1, 0:512]),
                    waits=((s_ev, s_ev.n),), inc=(s_ev, 1))
                p.on("vector", lambda e: e.tensor_sub(
                    r0[0:1, 512:1024], r0[0:1, 512:1024], r0[0:1, 1536:2048]),
                    waits=((s_ev, s_ev.n),), inc=(s_ev, 1))
                p.on("vector", lambda e: e.tensor_scalar(
                    r0[0:1, 512:1024], r0[0:1, 512:1024], float(EPS), None, ALU.add),
                    waits=((s_ev, s_ev.n),), inc=(s_ev, 1))
                EV_V = s_ev.n
                p.on("scalar", lambda e: e.activation(
                    out=r0[0:1, 1536:2048], in_=r0[0:1, 512:1024],
                    func=ACT.Sqrt),
                    waits=((s_ev, EV_V),), inc=(s_sc, 1))
                SC_SD = s_sc.n
                p.on("vector", lambda e: e.reciprocal(
                    r0[0:1, 1024:1536], r0[0:1, 1536:2048]),
                    waits=((s_sc, SC_SD),), inc=(s_ev, 1))
                SC_R = s_ev.n
                _bcast_col(p, t, ps, s_mm, r0[0:1, 0:512], 6, 512,
                           waits=((s_ev, SC_R),))
                _bcast_col(p, t, ps, s_mm, r0[0:1, 1024:1536], 7, 512)
                MM_B = s_mm.n
                for c in range(DC):
                    p.on("vector", lambda e, c=c, a=a, b=b: e.tensor_sub(
                        t["scr"].ap()[:, 0:512], rchunk[:, c, :], ps[:, 6, :]),
                        waits=((s_mm, MM_B),) if c == 0 else ())
                    p.on("vector", lambda e: e.tensor_mul(
                        t["scr"].ap()[:, 0:512], t["scr"].ap()[:, 0:512], ps[:, 7, :]))
                    p.on("vector", lambda e, c=c, a=a, b=b: e.tensor_copy(
                        t["xn_hi"].ap()[:, c, a:b], t["scr"].ap()[:, 0:512]))
                    p.on("vector", lambda e, c=c, a=a, b=b: e.tensor_sub(
                        t["xn_lo"].ap()[:, c, a:b], t["scr"].ap()[:, 0:512],
                        t["xn_hi"].ap()[:, c, a:b]), inc=(s_ev, 1))
            p.barrier()

        if phase_limit >= 1:
            emit_layernorm((s_st, ST_RESID))

        def emit_attention():
            wstage = pv(0, [128, 2, 8, 128])
            wtm_hi = pv(2048, [128, 1024], BF16)
            wtm_lo = pv(2560, [128, 1024], BF16)
            ptile = pv(2048, [128, 2, 2, 512], BF16)
            wT = {k_: pv(3072 + i_ * 512, [128, 8, 128], BF16)
                  for i_, k_ in enumerate(("qh", "ql", "kh", "kl", "vh", "vl"))}
            qs32 = pv(6144, [128, 2048])
            y_hi = pv(6144, [128, 2048], BF16)
            y_lo = pv(7168, [128, 2048], BF16)
            q_hi = pv(8192, [128, 2048], BF16)
            q_lo = pv(9216, [128, 2048], BF16)
            k_hi = pv(10240, [128, 2048], BF16)
            k_lo = pv(11264, [128, 2048], BF16)
            v_hi = pv(12288, [128, 16, 2, 64], BF16)
            v_lo = pv(13312, [128, 16, 2, 64], BF16)
            a32 = pv(14336, [128, 2, 512])
            rb = pv(15360, [128, 512])
            bias = t["bias"].ap()
            brow = t["brow"].ap()
            onesf = t["onesf"].ap()
            xnh = t["xn_hi"].ap()
            xnl = t["xn_lo"].ap()

            for hp in range(8):
                PAIR_GUARD = s_mm.n
                for wi in range(3):
                    p.dma("sync", lambda e, wi=wi, hp=hp: e.dma_start(
                        out=wstage[:, wi % 2, :, :].rearrange("p c f -> p (c f)"),
                        in_=qkv_w[wi * D + hp * 128: wi * D + hp * 128 + 128, :]),
                        waits=((s_sc, s_sc.n), (s_ev, s_ev.n)) if wi >= 2 else
                              ((s_mm, PAIR_GUARD),),
                        inc=(s_wl, 16))
                    LW = s_wl.n
                    p.on("scalar", lambda e, wi=wi: e.activation(
                        out=wtm_hi, in_=wstage[:, wi % 2, :, :].rearrange("p c f -> p (c f)"),
                        func=ACT.Copy),
                        waits=((s_wl, LW), (s_mm, PAIR_GUARD)) + (((s_ld, s_ld.n),) if wi else ()),
                        inc=(s_sc, 1))
                    SC_HI = s_sc.n
                    p.on("vector", lambda e, wi=wi: e.tensor_sub(
                        wtm_lo, wstage[:, wi % 2, :, :].rearrange("p c f -> p (c f)"), wtm_hi),
                        waits=((s_sc, SC_HI), (s_mm, PAIR_GUARD)), inc=(s_ev, 1))
                    EV_LO = s_ev.n
                    p.dma("sync", lambda e, wi=wi: e.dma_start(
                        out=wT[("qh", "kh", "vh")[wi]],
                        in_=wtm_hi.rearrange("p (c f) -> p c f", c=8), transpose=True),
                        waits=((s_sc, SC_HI),), inc=(s_ld, 16))
                    p.dma("sync", lambda e, wi=wi: e.dma_start(
                        out=wT[("ql", "kl", "vl")[wi]],
                        in_=wtm_lo.rearrange("p (c f) -> p c f", c=8), transpose=True),
                        waits=((s_ev, EV_LO),), inc=(s_ld, 16))
                    p.dma("sync", lambda e, wi=wi, hp=hp: e.dma_start(
                        out=brow[0:1, 128 * wi:128 * (wi + 1)],
                        in_=qkv_b[wi * D + hp * 128: wi * D + hp * 128 + 128]),
                        inc=(s_wl, 16))
                LD_WT = s_ld.n
                LW_B = s_wl.n

                for which in range(2):
                    hilo = (("qh", "ql"), ("kh", "kl"))[which]
                    dsth, dstl = ((q_hi, q_lo), (k_hi, k_lo))[which]
                    for tg in range(4):
                        bank = 4 + tg % 2
                        MM0 = s_mm.n
                        first_w = ((s_ld, LD_WT), (s_wl, LW_B))
                        if which + tg > 0:
                            first_w = first_w + ((s_sc, s_sc.n),)
                        for ci, (xs, ws) in enumerate(
                                ((xnh, hilo[0]), (xnl, hilo[0]), (xnh, hilo[1]))):
                            for dc in range(DC):
                                st_ = (ci == 0 and dc == 0)
                                p.on("tensor", lambda e, xs=xs, ws=ws, dc=dc, tg=tg,
                                     bank=bank, st=st_: e.matmul(
                                         ps[:, bank, :], wT[ws][:, dc, :],
                                         xs[:, dc, 512 * tg:512 * (tg + 1)],
                                         start=st, stop=False),
                                     waits=first_w if st_ else ())
                        p.on("tensor", lambda e, which=which, bank=bank: e.matmul(
                            ps[:, bank, :], brow[0:1, 128 * which:128 * (which + 1)],
                            onesf[0:1, 0:512], start=False, stop=True),
                            inc=(s_mm, 1))
                        MM1 = s_mm.n
                        p.on("scalar", lambda e, tg=tg, bank=bank, which=which: e.activation(
                            out=qs32[:, 512 * tg:512 * (tg + 1)], in_=ps[:, bank, :],
                            func=ACT.Copy, scale=(0.125 if which == 0 else 1.0)),
                            waits=((s_mm, MM1),) + (((s_ev, s_ev.n),) if which == 1 else ()),
                            inc=(s_sc, 1))
                        SC_E = s_sc.n
                        p.on("vector", lambda e, tg=tg, dsth=dsth: e.tensor_copy(
                            dsth[:, 512 * tg:512 * (tg + 1)], qs32[:, 512 * tg:512 * (tg + 1)]),
                            waits=((s_sc, SC_E),))
                        p.on("vector", lambda e, tg=tg, dsth=dsth, dstl=dstl: e.tensor_sub(
                            dstl[:, 512 * tg:512 * (tg + 1)], qs32[:, 512 * tg:512 * (tg + 1)],
                            dsth[:, 512 * tg:512 * (tg + 1)]))
                        p.on("vector", lambda e, tg=tg: e.tensor_mul(
                            ptile.rearrange("p a b f -> p (a b) f")[:, tg, :],
                            qs32[:, 512 * tg:512 * (tg + 1)],
                            qs32[:, 512 * tg:512 * (tg + 1)]), inc=(s_ev, 1))
                        EV_SQ = s_ev.n
                        p.on("tensor", lambda e, tg=tg: e.matmul(
                            ps[0:2, 6, :], t["Eh"].ap(),
                            ptile.rearrange("p a b f -> p (a b) f")[:, tg, :],
                            start=True, stop=True),
                            waits=((s_ev, EV_SQ),) + (
                                ((s_ev, s_ev.n),) if (tg > 0 or which == 1) else ()),
                            inc=(s_mm, 1))
                        MM_E = s_mm.n
                        p.on("vector", lambda e, tg=tg: e.reduce_max(
                            bias[0:2, 4 + tg:5 + tg], ps[0:2, 6, :], AX.X),
                            waits=((s_mm, MM_E),), inc=(s_ev, 1))
                    p.on("vector", lambda e, which=which: e.reduce_max(
                        bias[0:2, 2 + which:3 + which], bias[0:2, 4:8], AX.X),
                        waits=((s_ev, s_ev.n),), inc=(s_ev, 1))
                p.on("vector", lambda e: e.tensor_mul(
                    bias[0:2, 4:5], bias[0:2, 2:3], bias[0:2, 3:4]),
                    waits=((s_ev, s_ev.n),), inc=(s_ev, 1))
                p.on("scalar", lambda e: e.activation(
                    out=bias[0:2, 5:6], in_=bias[0:2, 4:5], func=ACT.Sqrt),
                    waits=((s_ev, s_ev.n),), inc=(s_sc, 1))
                p.on("vector", lambda e: e.tensor_scalar(
                    bias[0:2, 6:7], bias[0:2, 5:6], -1.0, None, ALU.mult),
                    waits=((s_sc, s_sc.n),), inc=(s_ev, 1))
                p.on("tensor", lambda e: e.matmul(
                    ps[0:1, 7, 0:2], bias[0:2, 6:7], ident[0:2, 0:2],
                    start=True, stop=True),
                    waits=((s_ev, s_ev.n),), inc=(s_mm, 1))
                p.on("vector", lambda e: e.tensor_copy(
                    brow[0:1, 1536:1538], ps[0:1, 7, 0:2]),
                    waits=((s_mm, s_mm.n),), inc=(s_ev, 1))
                EV_CT = s_ev.n
                for h in range(2):
                    p.on("tensor", lambda e, h=h: e.matmul(
                        ps[:, 7, 2 + h:3 + h], onesf[0:1, 0:128],
                        brow[0:1, 1536 + h:1537 + h], start=True, stop=True),
                        waits=((s_ev, EV_CT),), inc=(s_mm, 1))
                MM_CB = s_mm.n
                p.on("vector", lambda e: e.tensor_copy(
                    bias[:, 0:2], ps[:, 7, 2:4]),
                    waits=((s_mm, MM_CB),), inc=(s_ev, 1))

                p.on("vector", lambda e: e.engine_nop(), waits=((s_mm, s_mm.n),),
                     inc=(s_ev, 1))
                EV_VA = s_ev.n
                for tc in range(16):
                    bank = 4 + tc % 2
                    MM0 = s_mm.n
                    w0 = ((s_ev, EV_VA), (s_ld, LD_WT))
                    if tc >= 2:
                        w0 = w0 + ((s_ev, s_ev.n),)
                    first = True
                    for dc in range(DC):
                        for (xs, ws) in ((xnh, "vh"), (xnl, "vh"), (xnh, "vl")):
                            p.on("tensor", lambda e, xs=xs, ws=ws, dc=dc, tc=tc,
                                 bank=bank, st=first: e.matmul(
                                     ps[:, bank, 0:128], xs[:, dc, 128 * tc:128 * (tc + 1)],
                                     wT[ws][:, dc, :], start=st, stop=False),
                                 waits=w0 if first else ())
                            first = False
                    p.on("tensor", lambda e, tc=tc, bank=bank: e.matmul(
                        ps[:, bank, 0:128], onesf[0:1, 0:128],
                        brow[0:1, 256:384], start=False, stop=True),
                        inc=(s_mm, 1))
                    MM1 = s_mm.n
                    for h in range(2):
                        p.on("vector", lambda e, tc=tc, h=h, bank=bank: e.tensor_copy(
                            v_hi[:, tc, h, 0:64], ps[:, bank, 64 * h:64 * (h + 1)]),
                            waits=((s_mm, MM1),) if h == 0 else ())
                        p.on("vector", lambda e, tc=tc, h=h, bank=bank: e.tensor_sub(
                            v_lo[:, tc, h, 0:64], ps[:, bank, 64 * h:64 * (h + 1)],
                            v_hi[:, tc, h, 0:64]),
                            inc=(s_ev, 1) if h == 1 else None)
                p.barrier()

                for qq in range(4):
                    qa, qb = 512 * qq, 512 * (qq + 1)
                    PV0 = s_mm.n
                    for kc in range(16):
                        for h in range(2):
                            qk_bank = 1 + h
                            MMQK0 = s_mm.n
                            wqk = ()
                            if kc >= 1 or qq > 0 or hp > 0:
                                wqk = ((s_sc, s_sc.n),)
                            for ci, (qsrc, ksrc) in enumerate(
                                    ((q_hi, k_hi), (q_lo, k_hi), (q_hi, k_lo))):
                                p.on("tensor", lambda e, qsrc=qsrc, ksrc=ksrc, h=h,
                                     kc=kc, qa=qa, qb=qb, qk_bank=qk_bank,
                                     st=(ci == 0), sp=(ci == 2): e.matmul(
                                         ps[:, qk_bank, :],
                                         ksrc[64 * h:64 * (h + 1), 128 * kc:128 * (kc + 1)],
                                         qsrc[64 * h:64 * (h + 1), qa:qb],
                                         start=st, stop=sp),
                                     waits=wqk if ci == 0 else (),
                                     inc=(s_mm, 1) if ci == 2 else None)
                            MMQK = s_mm.n
                            wexp = ((s_mm, MMQK),)
                            if kc >= 1 or qq > 0 or hp > 0:
                                wexp = wexp + ((s_ev, s_ev.n), (s_gp, s_gp.n))
                            p.on("scalar", lambda e, h=h, qk_bank=qk_bank: e.activation(
                                out=a32[:, h, :], in_=ps[:, qk_bank, :],
                                func=ACT.Exp, bias=bias[:, h:h + 1]),
                                waits=wexp, inc=(s_sc, 1))
                            SC_EXP = s_sc.n
                            whi = ((s_sc, SC_EXP),)
                            if kc >= 1:
                                whi = whi + ((s_mm, s_mm.n),)
                            p.on("vector", lambda e, h=h: e.tensor_copy(
                                ptile[:, h, 0, :], a32[:, h, :]),
                                waits=whi, inc=(s_ev, 1))
                            EV_PH = s_ev.n
                            p.on("gpsimd", lambda e, h=h: e.tensor_sub(
                                ptile[:, h, 1, :], a32[:, h, :], ptile[:, h, 0, :]),
                                waits=((s_ev, EV_PH),), inc=(s_gp, 1))
                            GP_PL = s_gp.n
                            for ci, (pt, vs) in enumerate(((0, v_hi), (1, v_hi), (0, v_lo))):
                                st_ = (kc == 0 and ci == 0)
                                sp_ = (kc == 15 and ci == 2)
                                wpv = ((s_gp, GP_PL),) if ci == 0 else ()
                                if ci == 0 and kc == 0:
                                    wpv = wpv + ((s_ev, s_ev.n),)
                                p.on("tensor", lambda e, pt=pt, vs=vs, h=h, kc=kc,
                                     st=st_, sp=sp_: e.matmul(
                                         ps[0 + 64 * h:64 + 64 * h, 0, :],
                                         vs[:, kc, h, 0:64], ptile[:, h, pt, :],
                                         start=st, stop=sp, tile_position=(0, 64 * h)),
                                     waits=wpv,
                                     inc=(s_mm, 1) if sp_ else None)
                                if ci < 2:
                                    st2 = (kc == 0 and ci == 0)
                                    sp2 = (kc == 15 and ci == 1)
                                    p.on("tensor", lambda e, pt=pt, h=h, kc=kc,
                                         st=st2, sp=sp2: e.matmul(
                                             ps[32 * h:32 * h + 1, 3, :],
                                             t["onesb"].ap(), ptile[:, h, pt, :],
                                             start=st, stop=sp,
                                             tile_position=(0, 32 * h)),
                                         inc=(s_mm, 1) if sp2 else None)
                    MM_PVE = s_mm.n
                    p.on("vector", lambda e: e.reciprocal(
                        a32[0:1, 0, :], ps[0:1, 3, :]), waits=((s_mm, MM_PVE),))
                    p.on("vector", lambda e: e.reciprocal(
                        a32[32:33, 0, :], ps[32:33, 3, :]), inc=(s_ev, 1))
                    EV_RC = s_ev.n
                    p.on("tensor", lambda e: e.matmul(
                        ps[0:64, 6, :], onesf[0:1, 0:64], a32[0:1, 0, :],
                        start=True, stop=True),
                        waits=((s_ev, EV_RC),), inc=(s_mm, 1))
                    p.on("tensor", lambda e: e.matmul(
                        ps[64:128, 6, :], onesf[32:33, 0:64], a32[32:33, 0, :],
                        start=True, stop=True, tile_position=(32, 64)),
                        inc=(s_mm, 1))
                    MM_RB = s_mm.n
                    p.on("vector", lambda e: e.tensor_copy(rb, ps[:, 6, :]),
                         waits=((s_mm, MM_RB),))
                    p.on("vector", lambda e: e.tensor_mul(
                        a32.rearrange("p a b -> p (a b)")[:, 0:512], ps[:, 0, :], rb))
                    p.on("vector", lambda e, qa=qa, qb=qb: e.tensor_copy(
                        y_hi[:, qa:qb], a32.rearrange("p a b -> p (a b)")[:, 0:512]))
                    p.on("vector", lambda e, qa=qa, qb=qb: e.tensor_sub(
                        y_lo[:, qa:qb], a32.rearrange("p a b -> p (a b)")[:, 0:512],
                        y_hi[:, qa:qb]), inc=(s_ev, 1))
                p.barrier()

                p.dma("sync", lambda e, hp=hp: e.dma_start(
                    out=wstage[:, 0, :, :],
                    in_=out_w[:, hp * 128:(hp + 1) * 128].rearrange(
                        "(c p) ecol -> p c ecol", p=128)),
                    inc=(s_wl, 16))
                LW_O = s_wl.n
                p.on("scalar", lambda e: e.activation(
                    out=wtm_hi, in_=wstage[:, 0, :, :].rearrange("p c f -> p (c f)"),
                    func=ACT.Copy), waits=((s_wl, LW_O),), inc=(s_sc, 1))
                SC_OH = s_sc.n
                p.on("vector", lambda e: e.tensor_sub(
                    wtm_lo, wstage[:, 0, :, :].rearrange("p c f -> p (c f)"), wtm_hi),
                    waits=((s_sc, SC_OH),), inc=(s_ev, 1))
                EV_OL = s_ev.n
                p.dma("sync", lambda e: e.dma_start(
                    out=wT["qh"], in_=wtm_hi.rearrange("p (c f) -> p c f", c=8),
                    transpose=True), waits=((s_sc, SC_OH),), inc=(s_ld, 16))
                p.dma("sync", lambda e: e.dma_start(
                    out=wT["ql"], in_=wtm_lo.rearrange("p (c f) -> p c f", c=8),
                    transpose=True), waits=((s_ev, EV_OL),), inc=(s_ld, 16))
                LD_WO = s_ld.n
                if hp == 0:
                    p.dma("sync", lambda e: e.dma_start(
                        out=brow[0:1, 384:1408], in_=out_b), inc=(s_wl, 16))
                LW_OB = s_wl.n
                for fc in range(DC):
                    for tg in range(4):
                        bank = 4 + (fc * 4 + tg) % 2
                        MM0 = s_mm.n
                        wfirst = ((s_ld, LD_WO), (s_wl, LW_OB), (s_ev, s_ev.n))
                        for ci, (ysrc, ws) in enumerate(
                                ((y_hi, "qh"), (y_lo, "qh"), (y_hi, "ql"))):
                            p.on("tensor", lambda e, ysrc=ysrc, ws=ws, fc=fc, tg=tg,
                                 bank=bank, st=(ci == 0): e.matmul(
                                     ps[:, bank, :], wT[ws][:, fc, :],
                                     ysrc[:, 512 * tg:512 * (tg + 1)],
                                     start=st, stop=False),
                                 waits=wfirst if ci == 0 else ())
                        if hp == 0:
                            p.on("tensor", lambda e, fc=fc, bank=bank: e.matmul(
                                ps[:, bank, :], brow[0:1, 384 + 128 * fc:384 + 128 * (fc + 1)],
                                onesf[0:1, 0:512], start=False, stop=True),
                                inc=(s_mm, 1))
                        else:
                            pass
                        if hp != 0:
                            p.on("tensor", lambda e, bank=bank: e.matmul(
                                ps[:, bank, :], brow[0:1, 1408:1536],
                                onesf[0:1, 0:512], start=False, stop=True),
                                inc=(s_mm, 1))
                        MM1 = s_mm.n
                        if hp == 0:
                            p.on("vector", lambda e, fc=fc, tg=tg, bank=bank: e.tensor_copy(
                                t["acc"].ap()[:, fc, 512 * tg:512 * (tg + 1)], ps[:, bank, :]),
                                waits=((s_mm, MM1),), inc=(s_ev, 1))
                        else:
                            p.on("vector", lambda e, fc=fc, tg=tg, bank=bank: e.tensor_tensor(
                                t["acc"].ap()[:, fc, 512 * tg:512 * (tg + 1)],
                                ps[:, bank, :],
                                t["acc"].ap()[:, fc, 512 * tg:512 * (tg + 1)], ALU.add),
                                waits=((s_mm, MM1),), inc=(s_ev, 1))
                p.barrier()

        if phase_limit >= 2:
            p.on("vector", lambda e: e.memset(t["brow"].ap()[0:1, 1408:1536], 0.0),
                 inc=(s_ev, 1))
            emit_attention()

        def emit_router():
            probs = pv(0, [128, 16, 64])
            cwtm = pv(1024, [128, 16, 64])
            rst = pv(7680, [128, 8, 64])
            rwh = pv(7168, [128, 8, 64], BF16)
            rwl = pv(7424, [128, 8, 64], BF16)
            cwT = pv(3072, [64, 2048])
            cwT_hi = pv(13312, [64, 2048], BF16)
            cwT_lo = pv(14336, [64, 2048], BF16)
            scr2 = pv(2048, [128, 2, 64])
            brow = t["brow"].ap()
            onesf = t["onesf"].ap()
            p.dma("sync", lambda e: e.dma_start(
                out=rst, in_=router_w.rearrange("(c p) e -> p c e", p=128)),
                waits=((s_mm, s_mm.n),), inc=(s_wl, 16))
            p.dma("sync", lambda e: e.dma_start(
                out=brow[0:1, 0:64], in_=router_b), inc=(s_wl, 16))
            LW_R = s_wl.n
            p.on("scalar", lambda e: e.activation(
                out=rwh.rearrange("p c e -> p (c e)"), in_=rst.rearrange("p c e -> p (c e)"),
                func=ACT.Copy), waits=((s_wl, LW_R), (s_ev, s_ev.n)), inc=(s_sc, 1))
            SC_RH = s_sc.n
            p.on("vector", lambda e: e.tensor_sub(
                rwl.rearrange("p c e -> p (c e)"), rst.rearrange("p c e -> p (c e)"),
                rwh.rearrange("p c e -> p (c e)")),
                waits=((s_sc, SC_RH),), inc=(s_ev, 1))
            EV_RW = s_ev.n
            for tc in range(16):
                bank = 4 + tc % 2
                wfirst = ((s_ev, EV_RW),)
                if tc >= 2:
                    wfirst = wfirst + ((s_ev, s_ev.n),)
                first = True
                for dc in range(DC):
                    for (xs, ws) in ((t["xn_hi"].ap(), rwh), (t["xn_lo"].ap(), rwh),
                                     (t["xn_hi"].ap(), rwl)):
                        p.on("tensor", lambda e, xs=xs, ws=ws, dc=dc, tc=tc, bank=bank,
                             st=first: e.matmul(
                                 ps[:, bank, 0:64], xs[:, dc, 128 * tc:128 * (tc + 1)],
                                 ws[:, dc, :], start=st, stop=False),
                             waits=wfirst if first else ())
                        first = False
                p.on("tensor", lambda e, bank=bank: e.matmul(
                    ps[:, bank, 0:64], onesf[0:1, 0:128], brow[0:1, 0:64],
                    start=False, stop=True), inc=(s_mm, 1))
                MM1 = s_mm.n
                p.on("vector", lambda e, bank=bank: e.reduce_max(
                    scr2[:, 0, 0:1], ps[:, bank, 0:64], AX.X),
                    waits=((s_mm, MM1),), inc=(s_ev, 1))
                p.on("vector", lambda e: e.tensor_scalar(
                    scr2[:, 0, 1:2], scr2[:, 0, 0:1], -1.0, None, ALU.mult),
                    waits=((s_ev, s_ev.n),), inc=(s_ev, 1))
                EV_NM = s_ev.n
                p.on("scalar", lambda e, tc=tc, bank=bank: e.activation(
                    out=probs[:, tc, :], in_=ps[:, bank, 0:64], func=ACT.Exp,
                    bias=scr2[:, 0, 1:2]),
                    waits=((s_ev, EV_NM),), inc=(s_sc, 1))
                SC_P = s_sc.n
                p.on("vector", lambda e, tc=tc: e.max(
                    out=scr2[:, 1, 0:8], in_=probs[:, tc, :]),
                    waits=((s_sc, SC_P),), inc=(s_ev, 1))
                p.on("vector", lambda e, tc=tc: e.tensor_scalar(
                    scr2[:, 0, 0:64], probs[:, tc, :], scr2[:, 1, 7:8], None, ALU.is_ge),
                    waits=((s_ev, s_ev.n),), inc=(s_ev, 1))
                p.on("vector", lambda e, tc=tc: e.tensor_mul(
                    cwtm[:, tc, :], probs[:, tc, :], scr2[:, 0, 0:64]),
                    waits=((s_ev, s_ev.n),), inc=(s_ev, 1))
                p.on("vector", lambda e, tc=tc: e.reduce_sum(
                    scr2[:, 0, 0:1], cwtm[:, tc, :], AX.X),
                    waits=((s_ev, s_ev.n),), inc=(s_ev, 1))
                p.on("vector", lambda e: e.reciprocal(
                    scr2[:, 0, 1:2], scr2[:, 0, 0:1]),
                    waits=((s_ev, s_ev.n),), inc=(s_ev, 1))
                p.on("vector", lambda e, tc=tc: e.tensor_scalar(
                    cwtm[:, tc, :], cwtm[:, tc, :], scr2[:, 0, 1:2], None, ALU.mult),
                    waits=((s_ev, s_ev.n),), inc=(s_ev, 1))
                EV_CW = s_ev.n
                p.on("tensor", lambda e, tc=tc: e.matmul(
                    ps[0:64, 6, 0:128], cwtm[:, tc, :], ident,
                    start=True, stop=True),
                    waits=((s_ev, EV_CW),) + (((s_ev, s_ev.n),) if tc > 0 else ()),
                    inc=(s_mm, 1))
                MM_T = s_mm.n
                p.on("vector", lambda e, tc=tc: e.tensor_copy(
                    cwT[:, 128 * tc:128 * (tc + 1)], ps[0:64, 6, 0:128]),
                    waits=((s_mm, MM_T),), inc=(s_ev, 1))
            p.on("vector", lambda e: e.tensor_copy(cwT_hi, cwT),
                 waits=((s_ev, s_ev.n),))
            p.on("vector", lambda e: e.tensor_sub(cwT_lo, cwT, cwT_hi), inc=(s_ev, 1))
            p.barrier()

        def emit_moe_dense():
            wst = pv(0, [128, 8, 256])
            w2st = pv(0, [128, 2, 1024])
            w1h = pv(2048, [128, 8, 256], BF16)
            w1l = pv(3072, [128, 8, 256], BF16)
            w2h = pv(4096, [128, 2, 1024], BF16)
            w2l = pv(5120, [128, 2, 1024], BF16)
            hph = pv(6144, [128, 2, 2048], BF16)
            hpl = pv(8192, [128, 2, 2048], BF16)
            silu32 = pv(10240, [128, 512])
            hcw32 = pv(10752, [128, 512])
            cwrh = pv(11264, [1, 2048], BF16)
            cwrl = pv(12288, [1, 2048], BF16)
            cwT_hi = pv(13312, [64, 2048], BF16)
            cwT_lo = pv(14336, [64, 2048], BF16)
            brow = t["brow"].ap()
            onesf = t["onesf"].ap()
            xnh, xnl = t["xn_hi"].ap(), t["xn_lo"].ap()
            acc = t["acc"].ap()

            for e0 in range(n_moe):
                MME = s_mm.n
                p.dma("sync", lambda e, e0=e0: e.dma_start(
                    out=cwrh, in_=cwT_hi[e0:e0 + 1, :]),
                    waits=((s_mm, MME), (s_ev, s_ev.n)), inc=(s_wl, 16))
                p.dma("sync", lambda e, e0=e0: e.dma_start(
                    out=cwrl, in_=cwT_lo[e0:e0 + 1, :]), inc=(s_wl, 16))
                p.dma("sync", lambda e, e0=e0: e.dma_start(
                    out=wst, in_=moe_w1[e0].rearrange("(c p) f -> p c f", p=128)),
                    waits=((s_ev, s_ev.n), (s_sc, s_sc.n)), inc=(s_wl, 16))
                p.dma("sync", lambda e, e0=e0: e.dma_start(
                    out=brow[0:1, 0:256], in_=moe_b1[e0]), inc=(s_wl, 16))
                LW1 = s_wl.n
                p.on("scalar", lambda e: e.activation(
                    out=w1h.rearrange("p c f -> p (c f)"), in_=wst.rearrange("p c f -> p (c f)"),
                    func=ACT.Copy), waits=((s_wl, LW1), (s_mm, MME)), inc=(s_sc, 1))
                SC1 = s_sc.n
                p.on("vector", lambda e: e.tensor_sub(
                    w1l.rearrange("p c f -> p (c f)"), wst.rearrange("p c f -> p (c f)"),
                    w1h.rearrange("p c f -> p (c f)")),
                    waits=((s_sc, SC1), (s_mm, MME)), inc=(s_ev, 1))
                EV1 = s_ev.n
                p.dma("sync", lambda e, e0=e0: e.dma_start(
                    out=w2st, in_=moe_w2[e0].rearrange("(c p) f -> p c f", p=128)),
                    waits=((s_sc, SC1), (s_ev, EV1)), inc=(s_wl, 16))
                LW2 = s_wl.n
                for fc in range(2):
                    for tg in range(4):
                        bank = 4 + (fc * 4 + tg) % 2
                        wfirst = ((s_ev, EV1), (s_wl, LW1), (s_sc, s_sc.n))
                        first = True
                        for dc in range(DC):
                            for (xs, ws) in ((xnh, w1h), (xnl, w1h), (xnh, w1l)):
                                p.on("tensor", lambda e, xs=xs, ws=ws, dc=dc, tg=tg,
                                     fc=fc, bank=bank, st=first: e.matmul(
                                         ps[:, bank, :], ws[:, dc, 128 * fc:128 * (fc + 1)],
                                         xs[:, dc, 512 * tg:512 * (tg + 1)],
                                         start=st, stop=False),
                                     waits=wfirst if first else ())
                                first = False
                        p.on("tensor", lambda e, fc=fc, bank=bank: e.matmul(
                            ps[:, bank, :], brow[0:1, 128 * fc:128 * (fc + 1)],
                            onesf[0:1, 0:512], start=False, stop=True), inc=(s_mm, 1))
                        MH = s_mm.n
                        cb = 6 + (fc * 4 + tg) % 2
                        p.on("tensor", lambda e, tg=tg, cb=cb: e.matmul(
                            ps[:, cb, :], t["onesbr"].ap(), cwrh[0:1, 512 * tg:512 * (tg + 1)],
                            start=True, stop=False),
                            waits=((s_wl, LW1), (s_ev, s_ev.n)))
                        p.on("tensor", lambda e, tg=tg, cb=cb: e.matmul(
                            ps[:, cb, :], t["onesbr"].ap(), cwrl[0:1, 512 * tg:512 * (tg + 1)],
                            start=False, stop=True), inc=(s_mm, 1))
                        MCB = s_mm.n
                        p.on("scalar", lambda e, bank=bank: e.activation(
                            out=silu32, in_=ps[:, bank, :], func=ACT.Silu),
                            waits=((s_mm, MH), (s_ev, s_ev.n)), inc=(s_sc, 1))
                        SCS = s_sc.n
                        p.on("vector", lambda e, cb=cb: e.tensor_tensor(
                            hcw32, silu32, ps[:, cb, :], ALU.mult),
                            waits=((s_sc, SCS), (s_mm, MCB)))
                        p.on("vector", lambda e, fc=fc, tg=tg: e.tensor_copy(
                            hph[:, fc, 512 * tg:512 * (tg + 1)], hcw32))
                        p.on("vector", lambda e, fc=fc, tg=tg: e.tensor_sub(
                            hpl[:, fc, 512 * tg:512 * (tg + 1)], hcw32,
                            hph[:, fc, 512 * tg:512 * (tg + 1)]), inc=(s_ev, 1))
                EVH = s_ev.n
                p.on("scalar", lambda e: e.activation(
                    out=w2h.rearrange("p c f -> p (c f)"),
                    in_=w2st.rearrange("p c f -> p (c f)"),
                    func=ACT.Copy), waits=((s_wl, LW2), (s_mm, s_mm.n)), inc=(s_sc, 1))
                SC2 = s_sc.n
                p.on("vector", lambda e: e.tensor_sub(
                    w2l.rearrange("p c f -> p (c f)"),
                    w2st.rearrange("p c f -> p (c f)"),
                    w2h.rearrange("p c f -> p (c f)")),
                    waits=((s_sc, SC2),), inc=(s_ev, 1))
                EV2 = s_ev.n
                for dc in range(DC):
                    for tg in range(4):
                        bank = 4 + (dc * 4 + tg) % 2
                        wfirst = ((s_ev, EV2), (s_ev, EVH), (s_ev, s_ev.n))
                        first = True
                        for hc in range(2):
                            for (hs, ws) in ((hph, w2h), (hpl, w2h), (hph, w2l)):
                                st_, sp_ = first, (hc == 1 and ws is w2l)
                                p.on("tensor", lambda e, hs=hs, ws=ws, hc=hc, dc=dc,
                                     tg=tg, bank=bank, st=st_, sp=sp_: e.matmul(
                                         ps[:, bank, :], ws[:, hc, 128 * dc:128 * (dc + 1)],
                                         hs[:, hc, 512 * tg:512 * (tg + 1)],
                                         start=st, stop=sp),
                                     waits=wfirst if first else (),
                                     inc=(s_mm, 1) if sp_ else None)
                                first = False
                        MW2 = s_mm.n
                        p.on("vector", lambda e, dc=dc, tg=tg, bank=bank, e0=e0: (
                            e.tensor_tensor(acc[:, dc, 512 * tg:512 * (tg + 1)],
                                            ps[:, bank, :],
                                            acc[:, dc, 512 * tg:512 * (tg + 1)], ALU.add)),
                            waits=((s_mm, MW2),), inc=(s_ev, 1))
            p.dma("sync", lambda e: e.dma_start(
                out=wst.rearrange("p c f -> p (c f)")[0:64, 0:1024], in_=moe_b2),
                waits=((s_sc, s_sc.n), (s_ev, s_ev.n)), inc=(s_wl, 16))
            LB2 = s_wl.n
            p.on("scalar", lambda e: e.activation(
                out=w1h.rearrange("p c f -> p (c f)")[0:64, 0:1024],
                in_=wst.rearrange("p c f -> p (c f)")[0:64, 0:1024],
                func=ACT.Copy), waits=((s_wl, LB2), (s_ev, s_ev.n)), inc=(s_sc, 1))
            SB2 = s_sc.n
            p.on("vector", lambda e: e.tensor_sub(
                w1l.rearrange("p c f -> p (c f)")[0:64, 0:1024],
                wst.rearrange("p c f -> p (c f)")[0:64, 0:1024],
                w1h.rearrange("p c f -> p (c f)")[0:64, 0:1024]),
                waits=((s_sc, SB2),), inc=(s_ev, 1))
            EVB2 = s_ev.n
            for dc in range(DC):
                for tg in range(4):
                    bank = 4 + (dc * 4 + tg) % 2
                    for ci, (bs, cs) in enumerate(
                            ((w1h, pv(13312, [64, 2048], BF16)),
                             (w1h, pv(14336, [64, 2048], BF16)),
                             (w1l, pv(13312, [64, 2048], BF16)))):
                        p.on("tensor", lambda e, bs=bs, cs=cs, dc=dc, tg=tg, bank=bank,
                             st=(ci == 0), sp=(ci == 2): e.matmul(
                                 ps[:, bank, :],
                                 bs.rearrange("p c f -> p (c f)")[0:64, 128 * dc:128 * (dc + 1)],
                                 cs[:, 512 * tg:512 * (tg + 1)],
                                 start=st, stop=sp),
                             waits=((s_ev, EVB2), (s_ev, s_ev.n)) if ci == 0 else (),
                             inc=(s_mm, 1) if ci == 2 else None)
                    MB2 = s_mm.n
                    p.on("vector", lambda e, dc=dc, tg=tg, bank=bank: e.tensor_tensor(
                        acc[:, dc, 512 * tg:512 * (tg + 1)], ps[:, bank, :],
                        acc[:, dc, 512 * tg:512 * (tg + 1)], ALU.add),
                        waits=((s_mm, MB2),), inc=(s_ev, 1))
            p.barrier()

        if phase_limit >= 3:
            emit_router()
        if phase_limit >= 4:
            emit_moe_dense()

        def emit_shared():
            swst = pv(0, [128, 2, 8, 128])
            swh = pv(2048, [128, 8, 128], BF16)
            swl = pv(2560, [128, 8, 128], BF16)
            hsh = pv(3072, [128, 16, 512], BF16)
            hsl = pv(7168, [128, 16, 512], BF16)
            silu32 = pv(11264, [128, 512])
            w2st = pv(11776, [128, 16, 128])
            w2h = pv(13824, [128, 16, 128], BF16)
            w2l = pv(14848, [128, 16, 128], BF16)
            brow = t["brow"].ap()
            onesf = t["onesf"].ap()
            r0 = t["row0"].ap()
            xnh, xnl = t["xn_hi"].ap(), t["xn_lo"].ap()
            acc = t["acc"].ap()
            for se in range(ES):
                p.dma("sync", lambda e, se=se: e.dma_start(out=r0[0:1, :], in_=sb1[se]),
                      waits=((s_mm, s_mm.n), (s_ev, s_ev.n)), inc=(s_wl, 16))
                p.dma("sync", lambda e, se=se: e.dma_start(
                    out=brow[0:1, 384:1408], in_=sb2[se]), inc=(s_wl, 16))
                LWB = s_wl.n
                for tg in range(4):
                    a, b = 512 * tg, 512 * (tg + 1)
                    for fc in range(16):
                        MME = s_mm.n
                        p.dma("sync", lambda e, se=se, fc=fc: e.dma_start(
                            out=swst[:, fc % 2, :, :],
                            in_=sw1[se][:, 128 * fc:128 * (fc + 1)].rearrange(
                                "(c p) f -> p c f", p=128)),
                            waits=((s_sc, s_sc.n), (s_ev, s_ev.n)), inc=(s_wl, 16))
                        LW1 = s_wl.n
                        p.on("scalar", lambda e, fc=fc: e.activation(
                            out=swh.rearrange("p c f -> p (c f)"),
                            in_=swst[:, fc % 2, :, :].rearrange("p c f -> p (c f)"),
                            func=ACT.Copy),
                            waits=((s_wl, LW1), (s_mm, MME), (s_ev, s_ev.n)), inc=(s_sc, 1))
                        SCH = s_sc.n
                        p.on("vector", lambda e, fc=fc: e.tensor_sub(
                            swl.rearrange("p c f -> p (c f)"),
                            swst[:, fc % 2, :, :].rearrange("p c f -> p (c f)"),
                            swh.rearrange("p c f -> p (c f)")),
                            waits=((s_sc, SCH),), inc=(s_ev, 1))
                        EVL = s_ev.n
                        bank = 4 + fc % 2
                        first = True
                        for dc in range(DC):
                            for (xs, ws) in ((xnh, swh), (xnl, swh), (xnh, swl)):
                                p.on("tensor", lambda e, xs=xs, ws=ws, dc=dc, a=a, b=b,
                                     bank=bank, st=first: e.matmul(
                                         ps[:, bank, :], ws[:, dc, :], xs[:, dc, a:b],
                                         start=st, stop=False),
                                     waits=((s_ev, EVL), (s_wl, LWB),
                                            (s_sc, s_sc.n)) if first else ())
                                first = False
                        p.on("tensor", lambda e, fc=fc, bank=bank: e.matmul(
                            ps[:, bank, :], r0[0:1, 128 * fc:128 * (fc + 1)],
                            onesf[0:1, 0:512], start=False, stop=True), inc=(s_mm, 1))
                        MMH = s_mm.n
                        p.on("scalar", lambda e, bank=bank: e.activation(
                            out=silu32, in_=ps[:, bank, :], func=ACT.Silu),
                            waits=((s_mm, MMH), (s_ev, s_ev.n)), inc=(s_sc, 1))
                        SCS = s_sc.n
                        p.on("vector", lambda e, fc=fc: e.tensor_copy(
                            hsh[:, fc, :], silu32), waits=((s_sc, SCS),))
                        p.on("vector", lambda e, fc=fc: e.tensor_sub(
                            hsl[:, fc, :], silu32, hsh[:, fc, :]), inc=(s_ev, 1))
                    EVHS = s_ev.n
                    for dc in range(DC):
                        MME = s_mm.n
                        p.dma("sync", lambda e, se=se, dc=dc: e.dma_start(
                            out=w2st,
                            in_=sw2[se][:, 128 * dc:128 * (dc + 1)].rearrange(
                                "(c p) f -> p c f", p=128)),
                            waits=((s_sc, s_sc.n), (s_ev, s_ev.n)), inc=(s_wl, 16))
                        LW2 = s_wl.n
                        p.on("scalar", lambda e: e.activation(
                            out=w2h.rearrange("p c f -> p (c f)"),
                            in_=w2st.rearrange("p c f -> p (c f)"),
                            func=ACT.Copy),
                            waits=((s_wl, LW2), (s_mm, MME)), inc=(s_sc, 1))
                        SC2 = s_sc.n
                        p.on("vector", lambda e: e.tensor_sub(
                            w2l.rearrange("p c f -> p (c f)"),
                            w2st.rearrange("p c f -> p (c f)"),
                            w2h.rearrange("p c f -> p (c f)")),
                            waits=((s_sc, SC2),), inc=(s_ev, 1))
                        EV2 = s_ev.n
                        bank = 4 + dc % 2
                        first = True
                        for hc in range(16):
                            for (hs_, ws) in ((hsh, w2h), (hsl, w2h), (hsh, w2l)):
                                p.on("tensor", lambda e, hs_=hs_, ws=ws, hc=hc, dc=dc,
                                     bank=bank, st=first: e.matmul(
                                         ps[:, bank, :], ws[:, hc, :], hs_[:, hc, :],
                                         start=st, stop=False),
                                     waits=((s_ev, EV2), (s_ev, EVHS),
                                            (s_ev, s_ev.n)) if first else ())
                                first = False
                        p.on("tensor", lambda e, dc=dc, bank=bank: e.matmul(
                            ps[:, bank, :], brow[0:1, 384 + 128 * dc:384 + 128 * (dc + 1)],
                            onesf[0:1, 0:512], start=False, stop=True), inc=(s_mm, 1))
                        MW2 = s_mm.n
                        p.on("vector", lambda e, dc=dc, a=a, b=b, bank=bank: e.tensor_tensor(
                            acc[:, dc, a:b], ps[:, bank, :], acc[:, dc, a:b], ALU.add),
                            waits=((s_mm, MW2),), inc=(s_ev, 1))
            p.barrier()

        def emit_block_close(first_block):
            EVA = s_ev.n
            for c in range(DC):
                p.dma("gpsimd", lambda e, c=c: e.dma_start(
                    out=resid_hbm.rearrange("(c p) t -> p c t", p=128)[:, c, :],
                    in_=t["acc"].ap()[:, c, :], accum_op=ALU.add),
                    waits=((s_ev, EVA),) if c == 0 else ((s_st, s_st.n),),
                    inc=(s_st, 16))
            return s_st.n

        if phase_limit >= 5:
            emit_shared()
        if phase_limit >= 6:
            ST_R1 = emit_block_close(True)
            emit_layernorm((s_st, ST_R1))

        def emit_gate_routing(parts=9):
            gst = pv(0, [128, 8, 8])
            gwh = pv(64, [128, 8, 8], BF16)
            gwl = pv(96, [128, 8, 8], BF16)
            ptile = pv(128, [128, 16, 8])
            pET = pv(2176, [8, 2048])
            los = pv(4224, [8, 4])
            msk = pv(4240, [8, 2048])
            cum = pv(6288, [8, 2048])
            cum2 = pv(8336, [8, 2048])
            offs = pv(10384, [8, 2048])
            offTM = pv(12432, [128, 16, 8])
            offI = pv(12432 + 2048, [128, 16, 8], I32)
            pairs = pv(14480 + 1024, [128, 2])
            tmb = pv(4240, [128, 16, 128], BF16)
            onesf = t["onesf"].ap()
            r0 = t["row0"].ap()
            xnh, xnl = t["xn_hi"].ap(), t["xn_lo"].ap()

            p.on("vector", lambda e: e.memset(r0[0:1, 0:1], 0.0),
                 waits=((s_mm, s_mm.n), (s_sc, s_sc.n)), inc=(s_ev, 1))
            for k in [1, 2, 4, 8, 16, 32, 64, 128, 256, 512, 1024]:
                p.on("vector", lambda e, k=k: e.tensor_scalar(
                    r0[0:1, k:2 * k], r0[0:1, 0:k], float(k), None, ALU.add),
                    waits=((s_ev, s_ev.n),), inc=(s_ev, 1))
            EV_IOTA2 = s_ev.n

            p.dma("sync", lambda e: e.dma_start(
                out=gst, in_=gate_w.rearrange("(c p) e -> p c e", p=128)),
                waits=((s_mm, s_mm.n),), inc=(s_wl, 16))
            LWG = s_wl.n
            p.on("scalar", lambda e: e.activation(
                out=gwh.rearrange("p c e -> p (c e)"), in_=gst.rearrange("p c e -> p (c e)"),
                func=ACT.Copy), waits=((s_wl, LWG), (s_ev, s_ev.n)), inc=(s_sc, 1))
            SCG = s_sc.n
            p.on("vector", lambda e: e.tensor_sub(
                gwl.rearrange("p c e -> p (c e)"), gst.rearrange("p c e -> p (c e)"),
                gwh.rearrange("p c e -> p (c e)")),
                waits=((s_sc, SCG),), inc=(s_ev, 1))
            EVG = s_ev.n
            for tc in range(16):
                bank = 4 + tc % 2
                first = True
                for dc in range(DC):
                    for (xs, ws) in ((xnh, gwh), (xnl, gwh), (xnh, gwl)):
                        st_ = first
                        p.on("tensor", lambda e, xs=xs, ws=ws, dc=dc, tc=tc, bank=bank,
                             st=st_, sp=(ws is gwl and dc == DC - 1): e.matmul(
                                 ps[:, bank, 0:8], xs[:, dc, 128 * tc:128 * (tc + 1)],
                                 ws[:, dc, :], start=st, stop=sp),
                             waits=((s_ev, EVG), (s_sc, s_sc.n)) if first else (),
                             inc=(s_mm, 1) if (ws is gwl and dc == DC - 1) else None)
                        first = False
                MMG = s_mm.n
                if parts == 0:
                    p.on("vector", lambda e, tc=tc, bank=bank: e.tensor_copy(
                        pv(4240, [128, 16, 8])[:, tc, :], ps[:, bank, 0:8]),
                        waits=((s_mm, MMG),), inc=(s_ev, 1))
                p.on("vector", lambda e, bank=bank: e.reduce_max(
                    los[0:8, 0:1] if False else pairs[:, 0:1], ps[:, bank, 0:8], AX.X),
                    waits=((s_mm, MMG),), inc=(s_ev, 1))
                p.on("vector", lambda e: e.tensor_scalar(
                    pairs[:, 1:2], pairs[:, 0:1], -1.0, None, ALU.mult),
                    waits=((s_ev, s_ev.n),), inc=(s_ev, 1))
                EVN = s_ev.n
                p.on("scalar", lambda e, tc=tc, bank=bank: e.activation(
                    out=ptile[:, tc, :], in_=ps[:, bank, 0:8], func=ACT.Exp,
                    bias=pairs[:, 1:2]),
                    waits=((s_ev, EVN),), inc=(s_sc, 1))
                SCE = s_sc.n
                p.on("vector", lambda e, tc=tc: e.reduce_sum(
                    pairs[:, 0:1], ptile[:, tc, :], AX.X),
                    waits=((s_sc, SCE),), inc=(s_ev, 1))
                p.on("vector", lambda e: e.reciprocal(pairs[:, 1:2], pairs[:, 0:1]),
                     waits=((s_ev, s_ev.n),), inc=(s_ev, 1))
                p.on("vector", lambda e, tc=tc: e.tensor_scalar(
                    ptile[:, tc, :], ptile[:, tc, :], pairs[:, 1:2], None, ALU.mult),
                    waits=((s_ev, s_ev.n),), inc=(s_ev, 1))
                EVP = s_ev.n
                p.on("tensor", lambda e, tc=tc: e.matmul(
                    ps[0:8, 6, 0:128], ptile[:, tc, :], ident, start=True, stop=True),
                    waits=((s_ev, EVP),) + (((s_ev, s_ev.n),) if tc > 0 else ()),
                    inc=(s_mm, 1))
                MMT = s_mm.n
                p.on("vector", lambda e, tc=tc: e.tensor_copy(
                    pET[0:8, 128 * tc:128 * (tc + 1)], ps[0:8, 6, 0:128]),
                    waits=((s_mm, MMT),), inc=(s_ev, 1))

            if parts < 2:
                p.barrier()
                return None
            p.on("vector", lambda e: e.tensor_copy(t["identb"].ap(), ident), inc=(s_ev, 1))
            EV_IDB = s_ev.n
            for tc in range(16):
                for k2, (src, dst) in enumerate(((xnh, xn_tm_hi), (xnl, xn_tm_lo))):
                    for dc in range(DC):
                        i = (tc * 2 + k2) * DC + dc
                        p.on("tensor", lambda e, src=src, dc=dc, tc=tc: e.matmul(
                            ps[:, dc % 4, 0:128], src[:, dc, 128 * tc:128 * (tc + 1)],
                            t["identb"].ap(), start=True, stop=True),
                            waits=((s_ev, EV_IDB),) + (((s_ev, s_ev.n),) if i >= 3 else ()),
                            inc=(s_mm, 1))
                        p.on("vector", lambda e, dc=dc: e.tensor_copy(
                            tmb[:, dc, :] if False else tmb.rearrange(
                                "p c f -> p (c f)")[:, 128 * dc:128 * (dc + 1)],
                            ps[:, dc % 4, 0:128]),
                            waits=((s_mm, s_mm.n),), inc=(s_ev, 1))
                    p.dma("sync", lambda e, tc=tc, dst=dst: e.dma_start(
                        out=dst[128 * tc:128 * (tc + 1), :],
                        in_=tmb.rearrange("p c f -> p (c f)")[:, 0:1024]),
                        waits=((s_ev, s_ev.n),), inc=(s_st, 16))
                    p.wait("tensor", s_st, s_st.n)
            ST_XNTM = s_st.n

            if parts < 3:
                p.barrier()
                return None
            p.on("vector", lambda e: e.memset(los[0:8, 0:1], 0.0),
                 waits=((s_ev, s_ev.n), (s_st, ST_XNTM)))
            p.on("vector", lambda e: e.memset(los[0:8, 1:2], 1.0), inc=(s_ev, 1))
            for it in range(64):
                p.on("vector", lambda e: e.tensor_tensor(
                    los[0:8, 2:3], los[0:8, 0:1], los[0:8, 1:2], ALU.add),
                    waits=((s_ev, s_ev.n),), inc=(s_ev, 1))
                p.on("vector", lambda e: e.tensor_scalar(
                    los[0:8, 2:3], los[0:8, 2:3], 0.5, None, ALU.mult),
                    waits=((s_ev, s_ev.n),), inc=(s_ev, 1))
                EVM = s_ev.n
                p.on("vector", lambda e: e.tensor_scalar(
                    msk[0:8, :], pET[0:8, :], los[0:8, 2:3], None, ALU.is_gt),
                    waits=((s_ev, EVM),), inc=(s_ev, 1))
                p.on("vector", lambda e: e.reduce_sum(
                    los[0:8, 3:4], msk[0:8, :], AX.X),
                    waits=((s_ev, s_ev.n),), inc=(s_ev, 1))
                EVC = s_ev.n
                p.on("vector", lambda e: e.tensor_scalar(
                    los[0:8, 3:4], los[0:8, 3:4], float(CAP), None, ALU.is_ge),
                    waits=((s_ev, EVC),), inc=(s_ev, 1))
                EVG2 = s_ev.n
                p.on("vector", lambda e: e.tensor_tensor(
                    pairs[0:8, 0:1], los[0:8, 2:3], los[0:8, 0:1], ALU.subtract),
                    waits=((s_ev, EVG2),))
                p.on("vector", lambda e: e.tensor_tensor(
                    pairs[0:8, 0:1], pairs[0:8, 0:1], los[0:8, 3:4], ALU.mult),
                    waits=((s_ev, s_ev.n),), inc=(s_ev, 1))
                p.on("vector", lambda e: e.tensor_tensor(
                    los[0:8, 0:1], los[0:8, 0:1], pairs[0:8, 0:1], ALU.add),
                    waits=((s_ev, s_ev.n),), inc=(s_ev, 1))
                p.on("vector", lambda e: e.tensor_tensor(
                    pairs[0:8, 0:1], los[0:8, 1:2], los[0:8, 2:3], ALU.subtract),
                    waits=((s_ev, s_ev.n),))
                p.on("vector", lambda e: e.tensor_tensor(
                    pairs[0:8, 0:1], pairs[0:8, 0:1], los[0:8, 3:4], ALU.mult),
                    waits=((s_ev, s_ev.n),), inc=(s_ev, 1))
                p.on("vector", lambda e: e.tensor_tensor(
                    los[0:8, 1:2], los[0:8, 2:3], pairs[0:8, 0:1], ALU.add),
                    waits=((s_ev, s_ev.n),), inc=(s_ev, 1))
            EV_BIS = s_ev.n
            p.on("vector", lambda e: e.tensor_scalar(
                msk[0:8, :], pET[0:8, :], los[0:8, 0:1], None, ALU.is_gt),
                waits=((s_ev, EV_BIS),), inc=(s_ev, 1))
            p.on("vector", lambda e: e.tensor_copy(cum[0:8, :], msk[0:8, :]),
                 waits=((s_ev, s_ev.n),), inc=(s_ev, 1))
            src_, dst_ = cum, cum2
            for k in [1, 2, 4, 8, 16, 32, 64, 128, 256, 512, 1024]:
                p.on("vector", lambda e, k=k, src=src_, dst=dst_: e.tensor_copy(
                    dst[0:8, 0:k], src[0:8, 0:k]),
                    waits=((s_ev, s_ev.n),))
                p.on("vector", lambda e, k=k, src=src_, dst=dst_: e.tensor_tensor(
                    dst[0:8, k:2048], src[0:8, k:2048], src[0:8, 0:2048 - k], ALU.add),
                    inc=(s_ev, 1))
                src_, dst_ = dst_, src_
            cumf = src_
            iota8 = dst_
            EV_CUM = s_ev.n
            for tg in range(4):
                p.on("tensor", lambda e, tg=tg: e.matmul(
                    ps[0:8, 6, :], onesf[0:1, 0:8], r0[0:1, 512 * tg:512 * (tg + 1)],
                    start=True, stop=True),
                    waits=((s_ev, EV_CUM),) + (((s_ev, s_ev.n),) if tg else ()),
                    inc=(s_mm, 1))
                MMI = s_mm.n
                p.on("vector", lambda e, tg=tg, iota8=iota8: e.tensor_copy(
                    iota8[0:8, 512 * tg:512 * (tg + 1)], ps[0:8, 6, :]),
                    waits=((s_mm, MMI),), inc=(s_ev, 1))
            EV_I8 = s_ev.n
            p.on("vector", lambda e, cumf=cumf: e.tensor_scalar(
                offs[0:8, :], cumf[0:8, :], -1.0, 511.0, ALU.add, ALU.min),
                waits=((s_ev, EV_I8),))
            p.on("vector", lambda e, cumf=cumf, iota8=iota8: e.tensor_tensor(
                iota8[0:8, :], iota8[0:8, :], cumf[0:8, :], ALU.subtract))
            p.on("vector", lambda e, iota8=iota8: e.tensor_scalar(
                iota8[0:8, :], iota8[0:8, :], 512.0, None, ALU.add), inc=(s_ev, 1))
            EV_U = s_ev.n
            p.on("vector", lambda e, iota8=iota8: e.tensor_tensor(
                offs[0:8, :], offs[0:8, :], iota8[0:8, :], ALU.subtract),
                waits=((s_ev, EV_U),))
            p.on("vector", lambda e: e.tensor_tensor(
                offs[0:8, :], offs[0:8, :], msk[0:8, :], ALU.mult))
            p.on("vector", lambda e, iota8=iota8: e.tensor_tensor(
                offs[0:8, :], offs[0:8, :], iota8[0:8, :], ALU.add), inc=(s_ev, 1))
            EV_OFF = s_ev.n
            if parts < 4:
                p.barrier()
                return None
            for tc in range(16):
                p.on("tensor", lambda e, tc=tc: e.matmul(
                    ps[0:128, 7, 0:8], offs[0:8, 128 * tc:128 * (tc + 1)],
                    ident[0:8, 0:8], start=True, stop=True),
                    waits=((s_ev, EV_OFF),) + (((s_ev, s_ev.n),) if tc else ()),
                    inc=(s_mm, 1))
                MMO = s_mm.n
                p.on("vector", lambda e, tc=tc: e.tensor_copy(
                    offTM[:, tc, :], ps[:, 7, 0:8]),
                    waits=((s_mm, MMO),), inc=(s_ev, 1))
            EV_OTM = s_ev.n
            p.on("vector", lambda e: e.tensor_copy(
                offI.rearrange("p c e -> p (c e)"), offTM.rearrange("p c e -> p (c e)")),
                waits=((s_ev, EV_OTM),), inc=(s_ev, 1))
            EV_OI = s_ev.n
            p.on("vector", lambda e: e.memset(t["scr"].ap()[:, 0:320], 0.0), inc=(s_ev, 1))
            EV_Z = s_ev.n
            p.dma("gpsimd", lambda e: e.dma_start(
                out=slot_dram.rearrange("e c two -> (e c two)").rearrange(
                    "(p n) -> p n", p=128),
                in_=t["scr"].ap()[:, 0:320]),
                waits=((s_ev, EV_Z),), inc=(s_gp, 16))
            GP_Z = s_gp.n
            for tc in range(16):
                p.on("vector", lambda e, tc=tc: e.tensor_scalar(
                    pairs[:, 0:1], t["pidx"].ap(), float(128 * tc), None, ALU.add),
                    waits=((s_ev, s_ev.n), (s_gp, s_gp.n)), inc=(s_ev, 1))
                for e1_ in range(E1):
                    p.on("vector", lambda e, tc=tc, e1_=e1_: e.tensor_copy(
                        pairs[:, 1:2], ptile[:, tc, e1_:e1_ + 1]),
                        waits=((s_ev, s_ev.n), (s_gp, s_gp.n)), inc=(s_ev, 1))
                    EV_PAIR = s_ev.n
                    p.dma("gpsimd", lambda e, tc=tc, e1_=e1_: e.indirect_dma_start(
                        out=slot_dram,
                        out_offset=IndirectOffsetOnAxis(offI[:, tc, e1_:e1_ + 1], 1),
                        in_=pairs, in_offset=None,
                        element_offset=e1_ * 2560 * 2),
                        waits=((s_ev, EV_PAIR), (s_gp, GP_Z)), inc=(s_gp, 16))
            p.barrier()
            return ST_XNTM, EV_IOTA2, dict(msk=msk, cumf=cumf, offs=offs, dst=dst_,
                                           ptile=ptile, offTM=offTM, offI=offI,
                                           pairs=pairs, pET=pET, los=los)

        if phase_limit >= 7:
            emit_gate_routing(gate_parts)

        def emit_ssm(parts=9, scan_tiles=32):
            onesf = t["onesf"].ap()
            brow = t["brow"].ap()
            r0 = t["row0"].ap()
            kco = t["kco"].ap()
            pidx = t["pidx"].ap()
            xnh, xnl = t["xn_hi"].ap(), t["xn_lo"].ap()
            wstage = pv(0, [128, 1, 8, 128])
            zh = pv(0, [128, 2048], BF16)
            zl = pv(1024, [128, 2048], BF16)
            wtm_hi = pv(1024, [128, 1024], BF16)
            wtm_lo = pv(1536, [128, 1024], BF16)
            wTh = pv(2048, [128, 8, 128], BF16)
            wTl = pv(2560, [128, 8, 128], BF16)
            u32 = pv(3072, [128, 2048])
            c1 = pv(5120, [128, 2048])
            xb32 = pv(7168, [128, 4, 2048])

            kv = t["kco"].ap()[:, 0:32].rearrange("p (c k) -> p c k", k=8)
            p.dma("sync", lambda e, kv=kv: e.dma_start(
                out=kv[:, :, 0:3],
                in_=KxI.rearrange("(c p) one k -> p c (one k)", p=128)),
                waits=((s_mm, s_mm.n), (s_gp, s_gp.n)), inc=(s_wl, 16))
            p.dma("sync", lambda e, kv=kv: e.dma_start(
                out=kv[:, :, 4:7],
                in_=KzI.rearrange("(c p) one k -> p c (one k)", p=128)), inc=(s_wl, 16))
            LWK = s_wl.n

            def prep_wT(rows_ap):
                p.dma("sync", lambda e, rows_ap=rows_ap: e.dma_start(
                    out=wstage[:, 0, :, :].rearrange("p c f -> p (c f)"), in_=rows_ap),
                    waits=((s_sc, s_sc.n), (s_ev, s_ev.n), (s_st, s_st.n)),
                    inc=(s_wl, 16))
                LW = s_wl.n
                p.on("scalar", lambda e: e.activation(
                    out=wtm_hi, in_=wstage[:, 0, :, :].rearrange("p c f -> p (c f)"),
                    func=ACT.Copy), waits=((s_wl, LW), (s_ld, s_ld.n)), inc=(s_sc, 1))
                SCH = s_sc.n
                p.on("vector", lambda e: e.tensor_sub(
                    wtm_lo, wstage[:, 0, :, :].rearrange("p c f -> p (c f)"), wtm_hi),
                    waits=((s_sc, SCH),), inc=(s_ev, 1))
                EVL = s_ev.n
                p.dma("sync", lambda e: e.dma_start(
                    out=wTh, in_=wtm_hi.rearrange("p (c f) -> p c f", c=8),
                    transpose=True), waits=((s_sc, SCH),), inc=(s_ld, 16))
                p.dma("sync", lambda e: e.dma_start(
                    out=wTl, in_=wtm_lo.rearrange("p (c f) -> p c f", c=8),
                    transpose=True), waits=((s_ev, EVL),), inc=(s_ld, 16))
                return s_ld.n

            for ec in range(8):
                LDW = prep_wT(W_in[128 * ec:128 * (ec + 1), :])
                for tg in range(4):
                    bank = 4 + tg % 2
                    first = True
                    for dc in range(DC):
                        for (xs, ws) in ((xnh, wTh), (xnl, wTh), (xnh, wTl)):
                            sp_ = (dc == DC - 1 and ws is wTl)
                            p.on("tensor", lambda e, xs=xs, ws=ws, dc=dc, tg=tg,
                                 bank=bank, st=first, sp=sp_: e.matmul(
                                     ps[:, bank, :], ws[:, dc, :],
                                     xs[:, dc, 512 * tg:512 * (tg + 1)],
                                     start=st, stop=sp),
                                 waits=((s_ld, LDW), (s_sc, s_sc.n)) if first else (),
                                 inc=(s_mm, 1) if sp_ else None)
                            first = False
                    MH = s_mm.n
                    p.on("scalar", lambda e, tg=tg, bank=bank: e.activation(
                        out=u32[:, 512 * tg:512 * (tg + 1)], in_=ps[:, bank, :],
                        func=ACT.Copy), waits=((s_mm, MH), (s_ev, s_ev.n), (s_wl, LWK)),
                        inc=(s_sc, 1))
                SCU = s_sc.n
                cc, ko = ec % 4, (0 if ec < 4 else 4)
                p.on("vector", lambda e, cc=cc, ko=ko: e.tensor_scalar(
                    c1, u32, kv[:, cc, ko + 1:ko + 2], None, ALU.mult),
                    waits=((s_sc, SCU),), inc=(s_ev, 1))
                p.on("vector", lambda e, cc=cc, ko=ko: e.scalar_tensor_tensor(
                    c1[:, 1:2048], u32[:, 0:2047], kv[:, cc, ko + 0:ko + 1],
                    c1[:, 1:2048], ALU.mult, ALU.add),
                    waits=((s_ev, s_ev.n),), inc=(s_ev, 1))
                p.on("vector", lambda e, cc=cc, ko=ko: e.scalar_tensor_tensor(
                    c1[:, 0:2047], u32[:, 1:2048], kv[:, cc, ko + 2:ko + 3],
                    c1[:, 0:2047], ALU.mult, ALU.add),
                    waits=((s_ev, s_ev.n),), inc=(s_ev, 1))
                EVCV = s_ev.n
                if ec < 4:
                    p.on("scalar", lambda e, cc=cc: e.activation(
                        out=xb32[:, cc, :], in_=c1, func=ACT.Silu),
                        waits=((s_ev, EVCV),), inc=(s_sc, 1))
                    SCS = s_sc.n
                    p.dma("sync", lambda e, cc=cc: e.dma_start(
                        out=xb_hbm[128 * cc:128 * (cc + 1), :], in_=xb32[:, cc, :]),
                        waits=((s_sc, SCS),), inc=(s_st, 16))
                else:
                    p.on("scalar", lambda e: e.activation(
                        out=u32, in_=c1, func=ACT.Silu),
                        waits=((s_ev, EVCV),), inc=(s_sc, 1))
                    SCS = s_sc.n
                    p.on("vector", lambda e: e.tensor_copy(zh, u32),
                         waits=((s_sc, SCS), (s_wl, s_wl.n), (s_ld, s_ld.n)))
                    p.on("vector", lambda e: e.tensor_sub(zl, u32, zh), inc=(s_ev, 1))
                    EVZ = s_ev.n
                    p.dma("sync", lambda e, cc=cc: e.dma_start(
                        out=zb_hbm_hi[128 * cc:128 * (cc + 1), :], in_=zh),
                        waits=((s_ev, EVZ),), inc=(s_st, 16))
                    p.dma("sync", lambda e, cc=cc: e.dma_start(
                        out=zb_hbm_lo[128 * cc:128 * (cc + 1), :], in_=zl),
                        inc=(s_st, 16))
            ST_Z = s_st.n
            p.barrier()
            if parts < 2:
                return

            wx_s = pv(0, [80, 512])
            wxh = pv(512, [80, 512], BF16)
            wxl = pv(768, [80, 512], BF16)
            WxTh = pv(1024, [128, 4, 128], BF16)
            WxTl = pv(1280, [128, 4, 128], BF16)
            xbh = pv(1536, [128, 2048], BF16)
            xbl = pv(2560, [128, 2048], BF16)
            dth = pv(1536, [64, 2048], BF16)
            dtl = pv(2560, [64, 2048], BF16)
            proj32 = pv(3584, [128, 2048])
            breg = pv(7168, [128, 2048])
            wdt_s = pv(5632, [128, 4, 64])
            wdth_s = pv(5888, [128, 4, 64], BF16)
            wdtl_s = pv(6016, [128, 4, 64], BF16)
            WdTh = pv(6144, [64, 4, 128], BF16)
            WdTl = pv(6400, [64, 4, 128], BF16)
            bch = pv(9216, [128, 2048], BF16)
            bcl = pv(10240, [128, 2048], BF16)
            a32 = pv(15400, [128, 4, 8])
            aab = pv(15432, [128, 32])
            arel = pv(15464, [128, 32])
            anegt = pv(15360, [128, 32])
            Dv = pv(15392, [128, 4])
            dtile = pv(6656, [128, 512])
            drel = pv(11264, [128, 512])

            p.dma("sync", lambda e: e.dma_start(out=wx_s, in_=W_xproj),
                  waits=((s_mm, s_mm.n), (s_ev, s_ev.n), (s_sc, s_sc.n)), inc=(s_wl, 16))
            p.dma("sync", lambda e: e.dma_start(
                out=wdt_s, in_=W_dt.rearrange("(c p) r -> p c r", p=128)),
                inc=(s_wl, 16))
            p.dma("sync", lambda e: e.dma_start(
                out=brow[0:1, 0:512], in_=b_dt), inc=(s_wl, 16))
            p.dma("sync", lambda e: e.dma_start(
                out=a32, in_=A_log.rearrange("(c p) k -> p c k", p=128)),
                inc=(s_wl, 16))
            p.dma("sync", lambda e: e.dma_start(
                out=Dv, in_=Dvec.rearrange("(c p) -> p c", p=128),
                allow_slow_non_contiguous=True), inc=(s_wl, 16))
            LWX = s_wl.n
            p.on("scalar", lambda e: e.activation(out=wxh, in_=wx_s, func=ACT.Copy),
                 waits=((s_wl, LWX),), inc=(s_sc, 1))
            SCX = s_sc.n
            p.on("vector", lambda e: e.tensor_sub(wxl, wx_s, wxh),
                 waits=((s_sc, SCX),), inc=(s_ev, 1))
            p.on("scalar", lambda e: e.activation(
                out=wdth_s.rearrange("p c r -> p (c r)"),
                in_=wdt_s.rearrange("p c r -> p (c r)"), func=ACT.Copy),
                waits=((s_wl, LWX),), inc=(s_sc, 1))
            SCD = s_sc.n
            p.on("vector", lambda e: e.tensor_sub(
                wdtl_s.rearrange("p c r -> p (c r)"),
                wdt_s.rearrange("p c r -> p (c r)"),
                wdth_s.rearrange("p c r -> p (c r)")),
                waits=((s_sc, SCD),), inc=(s_ev, 1))
            EVWD = s_ev.n
            for cc2 in range(4):
                p.on("tensor", lambda e, cc2=cc2: e.matmul(
                    ps[0:128, 6, 0:128], wxh[:, 128 * cc2:128 * (cc2 + 1)],
                    t["identb"].ap()[0:80, 0:128], start=True, stop=True),
                    waits=((s_ev, s_ev.n),), inc=(s_mm, 1))
                MMX = s_mm.n
                p.on("vector", lambda e, cc2=cc2: e.tensor_copy(
                    WxTh[:, cc2, :], ps[:, 6, 0:128]),
                    waits=((s_mm, MMX),), inc=(s_ev, 1))
                p.on("tensor", lambda e, cc2=cc2: e.matmul(
                    ps[0:128, 6, 0:128], wxl[:, 128 * cc2:128 * (cc2 + 1)],
                    t["identb"].ap()[0:80, 0:128], start=True, stop=True),
                    waits=((s_ev, s_ev.n),), inc=(s_mm, 1))
                MMX2 = s_mm.n
                p.on("vector", lambda e, cc2=cc2: e.tensor_copy(
                    WxTl[:, cc2, :], ps[:, 6, 0:128]),
                    waits=((s_mm, MMX2),), inc=(s_ev, 1))
                p.on("tensor", lambda e, cc2=cc2: e.matmul(
                    ps[0:64, 7, 0:128], wdth_s[:, cc2, :], t["identb"].ap(),
                    start=True, stop=True),
                    waits=((s_ev, EVWD), (s_ev, s_ev.n)), inc=(s_mm, 1))
                MMD = s_mm.n
                p.on("vector", lambda e, cc2=cc2: e.tensor_copy(
                    WdTh[0:64, cc2, :], ps[0:64, 7, 0:128]),
                    waits=((s_mm, MMD),), inc=(s_ev, 1))
                p.on("tensor", lambda e, cc2=cc2: e.matmul(
                    ps[0:64, 7, 0:128], wdtl_s[:, cc2, :], t["identb"].ap(),
                    start=True, stop=True),
                    waits=((s_ev, s_ev.n),), inc=(s_mm, 1))
                MMD2 = s_mm.n
                p.on("vector", lambda e, cc2=cc2: e.tensor_copy(
                    WdTl[0:64, cc2, :], ps[0:64, 7, 0:128]),
                    waits=((s_mm, MMD2),), inc=(s_ev, 1))
            MMCHUNK = s_mm.n
            for cc2 in range(4):
                p.on("vector", lambda e, cc2=cc2: e.tensor_copy(
                    xbh, xb32[:, cc2, :]),
                    waits=((s_ev, s_ev.n), (s_mm, MMCHUNK)))
                p.on("vector", lambda e, cc2=cc2: e.tensor_sub(
                    xbl, xb32[:, cc2, :], xbh), inc=(s_ev, 1))
                EVSB = s_ev.n
                for tg in range(4):
                    for ci, (xs, ws) in enumerate(((xbh, WxTh), (xbl, WxTh), (xbh, WxTl))):
                        st_ = (cc2 == 0 and ci == 0)
                        sp_ = (cc2 == 3 and ci == 2)
                        last_of_chunk = (tg == 3 and ci == 2)
                        p.on("tensor", lambda e, xs=xs, ws=ws, cc2=cc2, tg=tg,
                             st=st_, sp=sp_: e.matmul(
                                 ps[:, 4 + tg, :], ws[:, cc2, :],
                                 xs[:, 512 * tg:512 * (tg + 1)], start=st, stop=sp),
                             waits=((s_ev, EVSB),) if ci == 0 else (),
                             inc=(s_mm, 1) if last_of_chunk else None)
                MMCHUNK = s_mm.n
            MMP = s_mm.n
            for tg in range(4):
                p.on("vector", lambda e, tg=tg: e.tensor_copy(
                    proj32[:, 512 * tg:512 * (tg + 1)], ps[:, 4 + tg, :]),
                    waits=((s_mm, MMP),), inc=(s_ev, 1))
            EVPJ = s_ev.n
            p.on("scalar", lambda e: e.activation(
                out=breg[64:80, :], in_=proj32[64:80, :], func=ACT.Tanh),
                waits=((s_ev, EVPJ),), inc=(s_sc, 1))
            SCT = s_sc.n
            p.on("vector", lambda e: e.tensor_copy(bch[64:80, :], breg[64:80, :]),
                 waits=((s_sc, SCT),))
            p.on("vector", lambda e: e.tensor_sub(
                bcl[64:80, :], breg[64:80, :], bch[64:80, :]), inc=(s_ev, 1))
            p.on("vector", lambda e: e.tensor_copy(dth[0:64, :], proj32[0:64, :]),
                 waits=((s_ev, s_ev.n),))
            p.on("vector", lambda e: e.tensor_sub(
                dtl[0:64, :], proj32[0:64, :], dth[0:64, :]), inc=(s_ev, 1))
            EVDT = s_ev.n
            p.on("scalar", lambda e: e.activation(
                out=aab,
                in_=a32.rearrange("p c k -> p (c k)"), func=ACT.Abs),
                waits=((s_wl, LWX),), inc=(s_sc, 1))
            p.on("scalar", lambda e: e.activation(
                out=aab, in_=aab, func=ACT.Exp, scale=-1.0),
                waits=((s_sc, s_sc.n),), inc=(s_sc, 1))
            p.on("scalar", lambda e: e.activation(
                out=aab, in_=aab, func=ACT.Ln, bias=t["kco"].ap()[:, 32:33]),
                waits=((s_sc, s_sc.n),), inc=(s_sc, 1))
            SCA = s_sc.n
            p.on("vector", lambda e: e.tensor_scalar(
                arel, a32.rearrange("p c k -> p (c k)"),
                0.0, None, ALU.max), waits=((s_sc, SCA),), inc=(s_ev, 1))
            p.on("vector", lambda e: e.tensor_tensor(
                aab, aab, arel, ALU.add), waits=((s_ev, s_ev.n),), inc=(s_ev, 1))
            p.on("vector", lambda e: e.tensor_scalar(
                aab, aab, 1e-4, -1.0, ALU.add, ALU.mult),
                waits=((s_ev, s_ev.n),), inc=(s_ev, 1))
            EVA = s_ev.n
            for j in range(32):
                cc2, base = j // 8, (16 * j) % 128
                p.dma("sync", lambda e, j=j, cc2=cc2, base=base: e.dma_start(
                    out=anegt[:, j:j + 1],
                    in_=aab.rearrange("p (c k) -> p c k", k=8)[base:base + 16, cc2, :]),
                    waits=((s_ev, EVA),) if j == 0 else (), inc=(s_wl, 16))
            LWA = s_wl.n
            for cc2 in range(4):
                for tg in range(4):
                    bank = 4 + (cc2 * 4 + tg) % 2
                    for ci, (xs, ws) in enumerate(((dth, WdTh), (dtl, WdTh), (dth, WdTl))):
                        st_, sp_ = (ci == 0), False
                        p.on("tensor", lambda e, xs=xs, ws=ws, cc2=cc2, tg=tg,
                             bank=bank, st=st_: e.matmul(
                                 ps[:, bank, :], ws[0:64, cc2, :],
                                 xs[0:64, 512 * tg:512 * (tg + 1)], start=st, stop=False),
                             waits=((s_ev, EVDT), (s_sc, s_sc.n)) if st_ else ())
                    p.on("tensor", lambda e, bank=bank, cc2=cc2: e.matmul(
                        ps[:, bank, :], brow[0:1, 128 * cc2:128 * (cc2 + 1)],
                        onesf[0:1, 0:512], start=False, stop=True), inc=(s_mm, 1))
                    MMDD = s_mm.n
                    p.on("scalar", lambda e, bank=bank: e.activation(
                        out=dtile, in_=ps[:, bank, :], func=ACT.Abs),
                        waits=((s_mm, MMDD), (s_st, s_st.n)), inc=(s_sc, 1))
                    p.on("scalar", lambda e: e.activation(
                        out=dtile, in_=dtile, func=ACT.Exp, scale=-1.0),
                        waits=((s_sc, s_sc.n),), inc=(s_sc, 1))
                    p.on("scalar", lambda e: e.activation(
                        out=dtile, in_=dtile, func=ACT.Ln, bias=t["kco"].ap()[:, 32:33]),
                        waits=((s_sc, s_sc.n),), inc=(s_sc, 1))
                    SCL = s_sc.n
                    p.on("vector", lambda e, bank=bank: e.tensor_scalar(
                        drel, ps[:, bank, :], 0.0, None, ALU.max),
                        waits=((s_sc, SCL),), inc=(s_ev, 1))
                    p.on("vector", lambda e: e.tensor_tensor(
                        dtile, dtile, drel, ALU.add),
                        waits=((s_ev, s_ev.n),), inc=(s_ev, 1))
                    p.on("vector", lambda e: e.tensor_scalar(
                        dtile, dtile, 1.0, 1e-4, ALU.min, ALU.max),
                        waits=((s_ev, s_ev.n),), inc=(s_ev, 1))
                    EVDL = s_ev.n
                    p.dma("sync", lambda e, cc2=cc2, tg=tg: e.dma_start(
                        out=delta_hbm[128 * cc2:128 * (cc2 + 1), 512 * tg:512 * (tg + 1)],
                        in_=dtile), waits=((s_ev, EVDL),), inc=(s_st, 16))
            ST_DELTA = s_st.n
            p.barrier()
            if parts < 3:
                return

            b_rep = pv(0, [128, 2048])
            c_rep = pv(2048, [128, 2048])
            ysh = pv(4096, [128, 2048], BF16)
            ysl = pv(5120, [128, 2048], BF16)
            xbsl = pv(6144, [128, 512])
            selb = pv(6656, [128, 128], BF16)
            iob32 = pv(6720, [128, 128])
            pdiv = pv(6784, [128, 8])
            Gev = pv(6792, [128, 32], BF16)
            God = pv(6808, [128, 32], BF16)
            selc = pv(6824, [128, 128], BF16)
            selbf = pv(6888, [128, 128])
            dl_rep = pv(7680, [128, 2048])
            decay = pv(9728, [128, 2048])
            bxr = pv(11776, [128, 2048])
            ubuf = pv(13824, [128, 2048])
            _bcast_col(p, t, ps, s_mm, r0[0:1, 0:128], 6, 128,
                       waits=((s_ev, s_ev.n), (s_sc, s_sc.n)))
            MMB = s_mm.n
            p.on("vector", lambda e: e.tensor_copy(iob32, ps[:, 6, 0:128]),
                 waits=((s_mm, MMB),), inc=(s_ev, 1))
            EVIO = s_ev.n
            if scan_tiles == -1:
                p.barrier()
                return
            p.on("vector", lambda e: e.memset(brow[0:1, 1408:1409], 0.0),
                 waits=((s_ev, s_ev.n),), inc=(s_ev, 1))
            for k in (1, 2, 4):
                p.on("vector", lambda e, k=k: e.tensor_scalar(
                    brow[0:1, 1408 + k:1408 + 2 * k], brow[0:1, 1408:1408 + k],
                    float(k), None, ALU.add),
                    waits=((s_ev, s_ev.n),), inc=(s_ev, 1))
            for k in (8, 16, 32, 64):
                p.on("vector", lambda e, k=k: e.tensor_copy(
                    brow[0:1, 1408 + k:1408 + 2 * k], brow[0:1, 1408:1408 + k]),
                    waits=((s_ev, s_ev.n),), inc=(s_ev, 1))
            EVM8R = s_ev.n
            _bcast_col(p, t, ps, s_mm, brow[0:1, 1408:1536], 7, 128,
                       waits=((s_ev, EVM8R),))
            MMB8 = s_mm.n
            iobm8 = selbf
            p.on("vector", lambda e: e.tensor_copy(iobm8, ps[:, 7, 0:128]),
                 waits=((s_mm, MMB8),), inc=(s_ev, 1))
            EVIM8 = s_ev.n
            if scan_tiles == -2:
                p.barrier()
                return
            p.on("vector", lambda e: e.tensor_scalar(
                pdiv[:, 2:3], pidx, -64.0, None, ALU.add),
                waits=((s_ev, EVIM8),), inc=(s_ev, 1))
            p.on("vector", lambda e: e.tensor_scalar(
                pdiv[:, 3:4], pidx, -72.0, None, ALU.add),
                waits=((s_ev, s_ev.n),), inc=(s_ev, 1))
            EVPD = s_ev.n
            gw = pv(13824, [128, 32])
            ga = pv(13856, [128, 32])
            p.on("vector", lambda e, gw=gw: e.tensor_scalar(
                gw, iob32[:, 0:32], 8.0, None, ALU.mult),
                waits=((s_ev, EVPD),), inc=(s_ev, 1))
            p.on("vector", lambda e, gw=gw: e.tensor_scalar(
                gw, gw, pidx, None, ALU.subtract),
                waits=((s_ev, s_ev.n),), inc=(s_ev, 1))
            p.on("vector", lambda e, gw=gw, ga=ga: e.tensor_scalar(
                ga, gw, -8.0, None, ALU.is_gt),
                waits=((s_ev, s_ev.n),), inc=(s_ev, 1))
            p.on("vector", lambda e, gw=gw: e.tensor_scalar(
                gw, gw, 0.0, None, ALU.is_le),
                waits=((s_ev, s_ev.n),), inc=(s_ev, 1))
            p.on("vector", lambda e, gw=gw, ga=ga: e.tensor_tensor(
                Gev, ga, gw, ALU.mult),
                waits=((s_ev, s_ev.n),), inc=(s_ev, 1))
            p.on("vector", lambda e, gw=gw: e.tensor_scalar(
                gw, iob32[:, 0:32], 8.0, -128.0, ALU.mult, ALU.add),
                waits=((s_ev, s_ev.n),), inc=(s_ev, 1))
            p.on("vector", lambda e, gw=gw: e.tensor_scalar(
                gw, gw, pidx, None, ALU.subtract),
                waits=((s_ev, s_ev.n),), inc=(s_ev, 1))
            p.on("vector", lambda e, gw=gw, ga=ga: e.tensor_scalar(
                ga, gw, -8.0, None, ALU.is_gt),
                waits=((s_ev, s_ev.n),), inc=(s_ev, 1))
            p.on("vector", lambda e, gw=gw: e.tensor_scalar(
                gw, gw, 0.0, None, ALU.is_le),
                waits=((s_ev, s_ev.n),), inc=(s_ev, 1))
            p.on("vector", lambda e, gw=gw, ga=ga: e.tensor_tensor(
                God, ga, gw, ALU.mult),
                waits=((s_ev, s_ev.n),), inc=(s_ev, 1))
            EVG = s_ev.n
            if scan_tiles == -3:
                p.barrier()
                return
            p.on("vector", lambda e: e.tensor_scalar(
                selb, iobm8, pdiv[:, 2:3], None, ALU.is_equal),
                waits=((s_ev, EVG),), inc=(s_ev, 1))
            p.on("vector", lambda e: e.tensor_scalar(
                pdiv[:, 4:5], pidx, 72.0, None, ALU.is_lt),
                waits=((s_ev, s_ev.n),), inc=(s_ev, 1))
            p.on("vector", lambda e: e.tensor_scalar(
                selb, selb, pdiv[:, 4:5], None, ALU.mult),
                waits=((s_ev, s_ev.n),), inc=(s_ev, 1))
            p.on("vector", lambda e: e.tensor_scalar(
                selc, iobm8, pdiv[:, 3:4], None, ALU.is_equal),
                waits=((s_ev, s_ev.n),), inc=(s_ev, 1))
            EVSEL = s_ev.n
            if scan_tiles == -4:
                p.barrier()
                return
            EVR = EVSEL
            for di, (dst, sel) in enumerate(((b_rep, selb), (c_rep, selc))):
                for tg in range(4):
                    for ci, src in enumerate((bch, bcl)):
                        p.on("tensor", lambda e, dst=dst, sel=sel, tg=tg, src=src,
                             st=(ci == 0), sp=(ci == 1): e.matmul(
                                 ps[:, 6, :], sel[64:80, :],
                                 src[64:80, 512 * tg:512 * (tg + 1)],
                                 start=st, stop=sp, tile_position=(64, 0)),
                             waits=((s_ev, EVR),) if ci == 0 else (),
                             inc=(s_mm, 1) if ci == 1 else None)
                    MMR = s_mm.n
                    p.on("vector", lambda e, dst=dst, tg=tg: e.tensor_copy(
                        dst[:, 512 * tg:512 * (tg + 1)], ps[:, 6, :]),
                        waits=((s_mm, MMR),), inc=(s_ev, 1))
                    EVR = s_ev.n
                if scan_tiles == -5 and di == 0:
                    p.barrier()
                    return
            EVREP = s_ev.n
            anegt2 = pv(7016, [128, 32])
            Dv2 = pv(7048, [128, 4])
            p.on("vector", lambda e: e.tensor_copy(anegt2, anegt),
                 waits=((s_wl, LWA), (s_ev, s_ev.n)))
            p.on("vector", lambda e: e.tensor_copy(Dv2, Dv), inc=(s_ev, 1))
            MMG = s_mm.n
            for cc2 in range(4):
                for k in range(8):
                    j = cc2 * 8 + k
                    if j >= scan_tiles:
                        break
                    base = (16 * j) % 128
                    LD0 = s_wl.n
                    for n in range(8):
                        p.dma("sync", lambda e, j=j, n=n: e.dma_start(
                            out=dl_rep[n:128:8, :],
                            in_=delta_hbm[16 * j:16 * j + 16, :]),
                            waits=((s_ev, s_ev.n), (s_st, ST_DELTA)) if n == 0 else (),
                            inc=(s_wl, 16))
                        p.dma("sync", lambda e, j=j, n=n: e.dma_start(
                            out=bxr[n:128:8, :], in_=xb_hbm[16 * j:16 * j + 16, :]),
                            inc=(s_wl, 16))
                    LDR = s_wl.n
                    p.on("scalar", lambda e, j=j: e.activation(
                        out=decay, in_=dl_rep, func=ACT.Exp, scale=anegt2[:, j:j + 1]),
                        waits=((s_wl, LDR), (s_wl, LWA), (s_ev, s_ev.n)), inc=(s_sc, 1))
                    SCDK = s_sc.n
                    p.on("vector", lambda e: e.tensor_tensor(
                        bxr, bxr, b_rep, ALU.mult),
                        waits=((s_sc, SCDK), (s_ev, EVREP)), inc=(s_ev, 1))
                    p.on("vector", lambda e: e.tensor_tensor(
                        ubuf, decay, bxr, ALU.mult),
                        waits=((s_ev, s_ev.n),), inc=(s_ev, 1))
                    p.on("vector", lambda e: e.tensor_tensor(
                        bxr, bxr, ubuf, ALU.subtract),
                        waits=((s_ev, s_ev.n),), inc=(s_ev, 1))
                    EVDR = s_ev.n
                    p.on("vector", lambda e: e.tensor_tensor_scan(
                        ubuf, decay, bxr, 0.0, ALU.mult, ALU.add),
                        waits=((s_ev, EVDR),), inc=(s_ev, 1))
                    p.on("vector", lambda e: e.tensor_tensor(
                        decay, ubuf, c_rep, ALU.mult),
                        waits=((s_ev, s_ev.n),), inc=(s_ev, 1))
                    p.on("vector", lambda e: e.tensor_copy(ysh, decay),
                         waits=((s_ev, s_ev.n), (s_mm, MMG), (s_st, s_st.n)))
                    p.on("vector", lambda e: e.tensor_sub(ysl, decay, ysh),
                         inc=(s_ev, 1))
                    EVYS = s_ev.n
                    stp = 32 * ((base // 32))
                    G = Gev if (k % 2 == 0) else God
                    for tg in range(4):
                        for ci, ysrc in enumerate((ysh, ysl)):
                            st_ = (k % 2 == 0 and ci == 0)
                            sp_ = (k % 2 == 1 and ci == 1)
                            inc_ = sp_ or (k % 2 == 0 and tg == 3 and ci == 1)
                            p.on("tensor", lambda e, G=G, ysrc=ysrc, tg=tg, stp=stp,
                                 st=st_, sp=sp_: e.matmul(
                                     ps[stp:stp + 32, tg, :], G,
                                     ysrc[:, 512 * tg:512 * (tg + 1)],
                                     start=st, stop=sp, tile_position=(0, stp)),
                                 waits=((s_ev, EVYS),) if ci == 0 else (),
                                 inc=(s_mm, 1) if inc_ else None)
                    MMG = s_mm.n
                MMY = s_mm.n
                if 8 * cc2 + 8 > scan_tiles:
                    break
                for tg in range(4):
                    p.dma("sync", lambda e, cc2=cc2, tg=tg: e.dma_start(
                        out=xbsl, in_=xb_hbm[128 * cc2:128 * (cc2 + 1),
                                            512 * tg:512 * (tg + 1)]),
                        waits=((s_ev, s_ev.n),), inc=(s_wl, 16))
                    LXB = s_wl.n
                    p.on("vector", lambda e, cc2=cc2, tg=tg: e.scalar_tensor_tensor(
                        dl_rep[:, 0:512], xbsl, Dv2[:, cc2:cc2 + 1], ps[:, tg, :],
                        ALU.mult, ALU.add),
                        waits=((s_mm, MMY), (s_wl, LXB)), inc=(s_ev, 1))
                    p.on("vector", lambda e: e.tensor_copy(
                        ysh[:, 0:512], dl_rep[:, 0:512]),
                        waits=((s_ev, s_ev.n), (s_st, s_st.n)))
                    p.on("vector", lambda e: e.tensor_sub(
                        ysl[:, 0:512], dl_rep[:, 0:512], ysh[:, 0:512]),
                        inc=(s_ev, 1))
                    EVY2 = s_ev.n
                    p.dma("sync", lambda e, cc2=cc2, tg=tg: e.dma_start(
                        out=y_hbm_hi[128 * cc2:128 * (cc2 + 1), 512 * tg:512 * (tg + 1)],
                        in_=ysh[:, 0:512]), waits=((s_ev, EVY2),), inc=(s_st, 16))
                    p.dma("sync", lambda e, cc2=cc2, tg=tg: e.dma_start(
                        out=y_hbm_lo[128 * cc2:128 * (cc2 + 1), 512 * tg:512 * (tg + 1)],
                        in_=ysl[:, 0:512]), waits=((s_ev, s_ev.n),), inc=(s_st, 16))
            ST_Y = s_st.n
            p.barrier()
            if parts < 4:
                return

            cath = pv(4096, [128, 4, 2048], BF16)
            catl = pv(8192, [128, 4, 2048], BF16)
            for half in range(2):
                srcs = ((y_hbm_hi, y_hbm_lo) if half == 0 else (zb_hbm_hi, zb_hbm_lo))
                LD0 = s_wl.n
                p.dma("sync", lambda e, srcs=srcs: e.dma_start(
                    out=cath, in_=srcs[0].rearrange("(c p) t -> p c t", p=128)),
                    waits=((s_mm, s_mm.n), (s_st, ST_Y), (s_ev, s_ev.n),
                           (s_sc, s_sc.n)), inc=(s_wl, 16))
                p.dma("sync", lambda e, srcs=srcs: e.dma_start(
                    out=catl, in_=srcs[1].rearrange("(c p) t -> p c t", p=128)),
                    inc=(s_wl, 16))
                LDC = s_wl.n
                for ec in range(8):
                    LDW = prep_wT(W_out[128 * ec:128 * (ec + 1), :])
                    for tg in range(4):
                        bank = 4 + tg % 2
                        first = True
                        for cc2 in range(4):
                            gc = half * 4 + cc2
                            for (xs, ws) in ((cath, wTh), (catl, wTh), (cath, wTl)):
                                sp_ = (cc2 == 3 and ws is wTl)
                                p.on("tensor", lambda e, xs=xs, ws=ws, cc2=cc2, gc=gc,
                                     tg=tg, bank=bank, st=first, sp=sp_: e.matmul(
                                         ps[:, bank, :], ws[:, gc, :],
                                         xs[:, cc2, 512 * tg:512 * (tg + 1)],
                                         start=st, stop=sp),
                                     waits=((s_ld, LDW), (s_wl, LDC),
                                            (s_ev, s_ev.n)) if first else (),
                                     inc=(s_mm, 1) if sp_ else None)
                                first = False
                        MMW = s_mm.n
                        if half == 0:
                            p.on("vector", lambda e, ec=ec, tg=tg, bank=bank:
                                 e.tensor_copy(
                                     t["acc"].ap()[:, ec, 512 * tg:512 * (tg + 1)],
                                     ps[:, bank, :]),
                                 waits=((s_mm, MMW),), inc=(s_ev, 1))
                        else:
                            p.on("vector", lambda e, ec=ec, tg=tg, bank=bank:
                                 e.tensor_tensor(
                                     t["acc"].ap()[:, ec, 512 * tg:512 * (tg + 1)],
                                     ps[:, bank, :],
                                     t["acc"].ap()[:, ec, 512 * tg:512 * (tg + 1)],
                                     ALU.add),
                                 waits=((s_mm, MMW),), inc=(s_ev, 1))
            p.barrier()

        def emit_base_out():
            rch = pv(0, [128, 2, DC, 128])
            sum32 = pv(4096, [128, DC, 128])
            obuf = pv(6144, [128, 2, 1024])
            for tc in range(16):
                p.dma("sync", lambda e, tc=tc: e.dma_start(
                    out=rch[:, tc % 2, :, :],
                    in_=resid_hbm.rearrange("(c p) t -> p c t", p=128)[
                        :, :, 128 * tc:128 * (tc + 1)]),
                    waits=((s_mm, s_mm.n), (s_ev, s_ev.n)) if tc < 2 else
                          ((s_mm, s_mm.n),), inc=(s_wl, 16))
                LDR = s_wl.n
                p.on("vector", lambda e, tc=tc: e.tensor_tensor(
                    sum32, rch[:, tc % 2, :, :],
                    t["acc"].ap()[:, :, 128 * tc:128 * (tc + 1)],
                    ALU.add), waits=((s_wl, LDR), (s_st, s_st.n)), inc=(s_ev, 1))
                EVS = s_ev.n
                for c in range(DC):
                    p.on("tensor", lambda e, tc=tc, c=c: e.matmul(
                        ps[:, c % 4, 0:128], sum32[:, c, :], ident,
                        start=True, stop=True),
                        waits=((s_ev, EVS),) + (((s_ev, s_ev.n),) if c >= 0 else ()),
                        inc=(s_mm, 1))
                    p.on("vector", lambda e, tc=tc, c=c: e.tensor_copy(
                        obuf[:, tc % 2, 128 * c:128 * (c + 1)], ps[:, c % 4, 0:128]),
                        waits=((s_mm, s_mm.n),), inc=(s_ev, 1))
                p.dma("sync", lambda e, tc=tc: e.dma_start(
                    out=OUT[128 * tc:128 * (tc + 1), :], in_=obuf[:, tc % 2, :]),
                    waits=((s_ev, s_ev.n),), inc=(s_st, 16))
            p.barrier()
            return s_st.n

        def emit_l1ffn(st_base):
            sloti = pv(0, [128, 4, 2])
            slotIdx = pv(8, [128, 4], I32)
            tpcol = pv(12, [128, 4])
            fidx = pv(16, [128, 4])
            gath_h = pv(512, [128, 1024], BF16)
            gath_l = pv(1024, [128, 1024], BF16)
            xinTh = pv(2048, [128, 8, 512], BF16)
            xinTl = pv(4096, [128, 8, 512], BF16)
            wst = pv(6144, [128, 8, 256])
            w1h = pv(8192, [128, 8, 256], BF16)
            w1l = pv(9216, [128, 8, 256], BF16)
            w2st = pv(6144, [128, 2, 1024])
            w2h = pv(10240, [128, 2, 1024], BF16)
            w2l = pv(11264, [128, 2, 1024], BF16)
            hfg = pv(12288, [128, 2, 2, 256], BF16)
            hfg2 = pv(12800, [128, 2, 2, 256], BF16)
            ybuf = pv(13312, [128, 1024])
            g32 = pv(14336, [128, 256])
            for e1_ in range(E1):
                MME = s_mm.n
                p.dma("sync", lambda e, e1_=e1_: e.dma_start(
                    out=sloti, in_=slot_dram[e1_][0:512].rearrange(
                        "(c p) two -> p c two", p=128)),
                    waits=((s_mm, MME), (s_ev, s_ev.n), (s_gp, s_gp.n)), inc=(s_wl, 16))
                LSI = s_wl.n
                p.on("vector", lambda e: e.tensor_copy(fidx, sloti[:, :, 0]),
                     waits=((s_wl, LSI),), inc=(s_ev, 1))
                p.on("vector", lambda e: e.tensor_copy(tpcol, sloti[:, :, 1]),
                     inc=(s_ev, 1))
                EVSI = s_ev.n
                p.on("vector", lambda e: e.tensor_copy(slotIdx, fidx),
                    waits=((s_ev, EVSI),), inc=(s_ev, 1))
                EVIX = s_ev.n
                for sc in range(4):
                    p.dma("gpsimd", lambda e, sc=sc: e.indirect_dma_start(
                        out=gath_h, out_offset=None, in_=xn_tm_hi,
                        in_offset=IndirectOffsetOnAxis(slotIdx[:, sc:sc + 1], 0)),
                        waits=((s_ev, EVIX), (s_mm, s_mm.n)), inc=(s_gp, 16))
                    p.dma("gpsimd", lambda e, sc=sc: e.indirect_dma_start(
                        out=gath_l, out_offset=None, in_=xn_tm_lo,
                        in_offset=IndirectOffsetOnAxis(slotIdx[:, sc:sc + 1], 0)),
                        inc=(s_gp, 16))
                    GPG = s_gp.n
                    for dc in range(DC):
                        for k2, (g, xT) in enumerate(((gath_h, xinTh), (gath_l, xinTl))):
                            p.on("tensor", lambda e, g=g, dc=dc, k2=k2: e.matmul(
                                ps[:, (dc * 2 + k2) % 4, 0:128],
                                g[:, 128 * dc:128 * (dc + 1)], t["identb"].ap(),
                                start=True, stop=True),
                                waits=((s_gp, GPG), (s_ev, s_ev.n)),
                                inc=(s_mm, 1))
                            MMT = s_mm.n
                            p.on("vector", lambda e, xT=xT, dc=dc, sc=sc, k2=k2:
                                 e.tensor_copy(
                                     xT[:, dc, 128 * sc:128 * (sc + 1)],
                                     ps[:, (dc * 2 + k2) % 4, 0:128]),
                                 waits=((s_mm, MMT),), inc=(s_ev, 1))
                EVXT = s_ev.n
                for half in range(2):
                    hs0 = 256 * half
                    for fg in range(16):
                        MMF = s_mm.n
                        p.dma("sync", lambda e, e1_=e1_, fg=fg: e.dma_start(
                            out=wst, in_=ffn1_w1[e1_][:, 256 * fg:256 * (fg + 1)]
                            .rearrange("(c p) f -> p c f", p=128)),
                            waits=((s_sc, s_sc.n), (s_ev, s_ev.n)), inc=(s_wl, 16))
                        LW1 = s_wl.n
                        p.on("scalar", lambda e: e.activation(
                            out=w1h.rearrange("p c f -> p (c f)"),
                            in_=wst.rearrange("p c f -> p (c f)"), func=ACT.Copy),
                            waits=((s_wl, LW1), (s_mm, MMF)), inc=(s_sc, 1))
                        SC1 = s_sc.n
                        p.on("vector", lambda e: e.tensor_sub(
                            w1l.rearrange("p c f -> p (c f)"),
                            wst.rearrange("p c f -> p (c f)"),
                            w1h.rearrange("p c f -> p (c f)")),
                            waits=((s_sc, SC1),), inc=(s_ev, 1))
                        EV1 = s_ev.n
                        p.dma("sync", lambda e, e1_=e1_, fg=fg: e.dma_start(
                            out=w2st, in_=ffn1_w2[e1_][256 * fg:256 * (fg + 1), :]
                            .rearrange("(c p) f -> p c f", p=128)),
                            waits=((s_sc, SC1), (s_ev, EV1)), inc=(s_wl, 16))
                        LW2 = s_wl.n
                        for fc in range(2):
                            hb = 6 + fc % 2
                            first = True
                            for dc in range(DC):
                                for (xs, ws) in ((xinTh, w1h), (xinTl, w1h),
                                                 (xinTh, w1l)):
                                    sp_ = (dc == DC - 1 and ws is w1l)
                                    p.on("tensor", lambda e, xs=xs, ws=ws, dc=dc,
                                         fc=fc, hb=hb, hs0=hs0, st=first, sp=sp_:
                                         e.matmul(
                                             ps[:, hb, 0:256],
                                             ws[:, dc, 128 * fc:128 * (fc + 1)],
                                             xs[:, dc, hs0:hs0 + 256],
                                             start=st, stop=sp),
                                         waits=((s_ev, EV1), (s_ev, EVXT),
                                                (s_sc, s_sc.n)) if first else (),
                                         inc=(s_mm, 1) if sp_ else None)
                                    first = False
                            MMH = s_mm.n
                            p.on("scalar", lambda e, hb=hb: e.activation(
                                out=g32, in_=ps[:, hb, 0:256], func=ACT.Gelu),
                                waits=((s_mm, MMH), (s_ev, s_ev.n)), inc=(s_sc, 1))
                            SCG = s_sc.n
                            p.on("vector", lambda e, fc=fc, fg=fg: e.tensor_copy(
                                hfg[:, fg % 2, fc, :], g32), waits=((s_sc, SCG),))
                            p.on("vector", lambda e, fc=fc, fg=fg: e.tensor_sub(
                                hfg2[:, fg % 2, fc, :], g32, hfg[:, fg % 2, fc, :]),
                                inc=(s_ev, 1))
                        EVH = s_ev.n
                        p.on("scalar", lambda e: e.activation(
                            out=w2h.rearrange("p c f -> p (c f)"),
                            in_=w2st.rearrange("p c f -> p (c f)"), func=ACT.Copy),
                            waits=((s_wl, LW2),), inc=(s_sc, 1))
                        SC2 = s_sc.n
                        p.on("vector", lambda e: e.tensor_sub(
                            w2l.rearrange("p c f -> p (c f)"),
                            w2st.rearrange("p c f -> p (c f)"),
                            w2h.rearrange("p c f -> p (c f)")),
                            waits=((s_sc, SC2),), inc=(s_ev, 1))
                        EV2 = s_ev.n
                        for fc in range(2):
                            for scl in range(2):
                                for dh in range(2):
                                    yb = scl * 2 + dh
                                    for ci, (hsrc, ws) in enumerate(
                                            ((hfg, w2h), (hfg2, w2h), (hfg, w2l))):
                                        st_ = (fg == 0 and fc == 0 and ci == 0)
                                        sp_ = (fg == 15 and fc == 1 and ci == 2)
                                        p.on("tensor", lambda e, hsrc=hsrc, ws=ws,
                                             fc=fc, scl=scl, dh=dh, yb=yb, fg=fg,
                                             st=st_, sp=sp_: e.matmul(
                                                 ps[:, yb, :],
                                                 hsrc[:, fg % 2, fc,
                                                      128 * scl:128 * (scl + 1)]
                                                 if False else
                                                 hsrc.rearrange(
                                                     "p b f s -> p b (f s)")[
                                                     :, fg % 2,
                                                     256 * fc + 128 * scl:
                                                     256 * fc + 128 * (scl + 1)],
                                                 ws[:, fc, 512 * dh:512 * (dh + 1)],
                                                 start=st, stop=sp),
                                             waits=(((s_ev, EVH), (s_ev, EV2))
                                                    + (((s_sc, s_sc.n),)
                                                       if st_ else ()))
                                             if ci == 0 else (),
                                             inc=(s_mm, 1) if sp_ else None)
                    MMYD = s_mm.n
                    for scl in range(2):
                        scg = half * 2 + scl
                        p.on("scalar", lambda e, scl=scl, scg=scg: e.activation(
                            out=ybuf[:, 0:512], in_=ps[:, scl * 2 + 0, :],
                            func=ACT.Copy, scale=tpcol[:, scg:scg + 1]),
                            waits=((s_mm, MMYD), (s_gp, s_gp.n)), inc=(s_sc, 1))
                        p.on("scalar", lambda e, scl=scl, scg=scg: e.activation(
                            out=ybuf[:, 512:1024], in_=ps[:, scl * 2 + 1, :],
                            func=ACT.Copy, scale=tpcol[:, scg:scg + 1]),
                            inc=(s_sc, 1))
                        SCY = s_sc.n
                        p.dma("gpsimd", lambda e, scg=scg: e.indirect_dma_start(
                            out=OUT, out_offset=IndirectOffsetOnAxis(
                                slotIdx[:, scg:scg + 1], 0),
                            in_=ybuf, in_offset=None, compute_op=ALU.add),
                            waits=((s_sc, SCY), (s_st, st_base), (s_gp, s_gp.n)),
                            inc=(s_gp, 16))
            p.wait("gpsimd", s_gp, s_gp.n)
            p.barrier(("vector", "scalar", "gpsimd"))

        if phase_limit >= 8:
            emit_ssm(ssm_parts, scan_tiles)
        if phase_limit >= 9:
            ST_BASE = emit_base_out()
            if ffn_on:
                emit_l1ffn(ST_BASE)

        dbg_names = []
        for (name, getter) in dump:
            apx = getter(tens, ps)
            o = nc.dram_tensor(f"dbg_{name}", list(apx.shape), apx.dtype,
                               kind="ExternalOutput").ap()
            if str(apx.space) == "MemorySpace.DRAM":
                total = 1
                for d_ in apx.shape:
                    total *= int(d_)
                assert total % 128 == 0
                flat_in = apx
                for _ in range(len(apx.shape) - 1):
                    flat_in = flat_in.flatten_outer_dims() if False else flat_in
                flat_in = apx.rearrange(
                    " ".join(f"d{i}" for i in range(len(apx.shape))) + " -> ("
                    + " ".join(f"d{i}" for i in range(len(apx.shape))) + ")")
                flat_in = flat_in.rearrange("(p n) -> p n", p=128)
                flat_o = o.rearrange(
                    " ".join(f"d{i}" for i in range(len(apx.shape))) + " -> ("
                    + " ".join(f"d{i}" for i in range(len(apx.shape))) + ")").rearrange(
                    "(p n) -> p n", p=128)
                W = total // 128
                for c0_ in range(0, W, 2048):
                    c1_ = min(c0_ + 2048, W)
                    p.dma("gpsimd", lambda e, a=flat_in[:, c0_:c1_],
                          strip=t["scr"].ap()[:, 0:c1_ - c0_]:
                          e.dma_start(out=strip, in_=a),
                          waits=((s_st, s_st.n),), inc=(s_st, 16))
                    p.dma("gpsimd", lambda e, ob=flat_o[:, c0_:c1_],
                          strip=t["scr"].ap()[:, 0:c1_ - c0_]:
                          e.dma_start(out=ob, in_=strip),
                          waits=((s_st, s_st.n),), inc=(s_st, 16))
            else:
                p.dma("gpsimd", lambda e, o=o, apx=apx: e.dma_start(out=o, in_=apx),
                      inc=(s_st, 16))
            dbg_names.append(name)

        if phase_limit < 9:
            p.dma("gpsimd", lambda e: e.dma_start(
                out=OUT.rearrange("t d -> (t d)").rearrange("(p n) -> p n", p=128),
                in_=t["acc"].ap().rearrange("p c t -> p (c t)")),
                waits=((s_ev, s_ev.n), (s_gp, s_gp.n)), inc=(s_st, 16))
        p.wait("gpsimd", s_st, s_st.n)

        p.finalize(block)
    return nc, dict(dbg=dbg_names)


def kernel(**inputs):
    nc, meta = build_program()
    base = {k: np.ascontiguousarray(np.asarray(v)) for k, v in inputs.items()}
    in_maps = []
    for c in range(NCORES):
        im = dict(base)
        im["L0_attn_x"] = np.ascontiguousarray(base["L0_attn_x"][c])
        in_maps.append(im)
    res = run_bass_kernel_spmd(nc, in_maps, core_ids=list(range(NCORES)))
    return np.stack([r["out"] for r in res.results])
